# version 01
# baseline (speedup 1.0000x reference)
import math
import contextlib
import numpy as np
import concourse.bass as bass
import concourse.mybir as mybir
from concourse.bass_utils import run_bass_kernel_spmd

F32 = mybir.dt.float32
BF16 = mybir.dt.bfloat16
U8 = mybir.dt.uint8
AF = mybir.ActivationFunctionType
OP = mybir.AluOpType

T = 2048
D = 4096
H = 32
HD = 128
NT = 16
ND = 32
NO2 = 16
K_TOP = 409.0
SHIFT = 16.0
SIGMA = 1.64
Z_P = 0.8416212335729143
C0 = SHIFT - math.log(2048.0) - SIGMA * SIGMA / 2 + Z_P * SIGMA
QSCALE = 1.0 / math.sqrt(128.0)
PC = [-111.33656853103689, 228.68416157385175, -180.57732498936406,
      69.0869892937092, -15.682627304359587, 2.317329186730592]
EPS = 1e-5
ARENA_B = 207872


def _rotl(x, d):
    return ((x << np.uint32(d)) | (x >> np.uint32(32 - d))).astype(np.uint32)


def _threefry2x32(k1, k2, x0, x1):
    ks0 = np.uint32(k1); ks1 = np.uint32(k2)
    ks2 = np.uint32(ks0 ^ ks1 ^ np.uint32(0x1BD11BDA))
    x0 = (x0 + ks0).astype(np.uint32); x1 = (x1 + ks1).astype(np.uint32)
    rot = [(13, 15, 26, 6), (17, 29, 16, 24)]
    keys = [(ks1, ks2), (ks2, ks0), (ks0, ks1), (ks1, ks2), (ks2, ks0)]
    for g in range(5):
        for r in rot[g % 2]:
            x0 = (x0 + x1).astype(np.uint32)
            x1 = _rotl(x1, r)
            x1 = (x1 ^ x0).astype(np.uint32)
        ka, kb = keys[g]
        x0 = (x0 + ka).astype(np.uint32)
        x1 = (x1 + kb + np.uint32(g + 1)).astype(np.uint32)
    return x0, x1


def make_sel():
    count = 8 * T * D
    i = np.arange(count, dtype=np.uint64)
    hi = (i >> np.uint64(32)).astype(np.uint32)
    lo = (i & np.uint64(0xFFFFFFFF)).astype(np.uint32)
    b1, b2 = _threefry2x32(np.uint32(0), np.uint32(42), hi, lo)
    return ((b1 ^ b2) >> np.uint32(30)).astype(np.uint8).reshape(8, T, D)


class Ctr:
    __slots__ = ("h", "n")

    def __init__(self, h):
        self.h = h
        self.n = 0


class Prog:
    ENGS = ("tensor", "vector", "scalar", "sync", "gpsimd")

    def __init__(self):
        self.streams = {e: [] for e in self.ENGS}

    def op(self, eng, fn, inc=None, k=1):
        if inc is None:
            self.streams[eng].append(fn)
            return
        inc.n += k
        kw = min(k, 16)
        if eng in ("vector", "scalar"):
            def w(e, fn=fn, h=inc.h, kw=kw):
                fn(e)
                e.drain().then_inc(h, kw)
            self.streams[eng].append(w)
        else:
            self.streams[eng].append(
                lambda e, fn=fn, h=inc.h, kw=kw: fn(e).then_inc(h, kw))

    def raw(self, eng, fn):
        self.streams[eng].append(fn)

    def wait(self, eng, ctr, val=None):
        v = ctr.n if val is None else val
        if v > 0:
            self.streams[eng].append(lambda e, h=ctr.h, v=v: e.wait_ge(h, v))

    def barrier(self, ctrs):
        for eng in self.ENGS:
            for c in ctrs:
                self.wait(eng, c)


class Layout:
    def __init__(self, base, limit):
        self.off = base
        self.limit = limit

    def take(self, nbytes, align=512):
        o = (self.off + align - 1) // align * align
        self.off = o + nbytes
        assert self.off <= self.limit, f"sbuf overflow {self.off} > {self.limit}"
        return o


def build_program(debug=False, phases=3, p1b_passes='qkvh', jmax=32, qstop='store', dump_g=False, trk=16):
    nc = bass.Bass("TRN2", target_bir_lowering=False)
    dm = {}
    dm["x"] = nc.dram_tensor("x", [T, D], F32, kind="ExternalInput").ap()
    dm["sel"] = nc.dram_tensor("sel", [T, D], U8, kind="ExternalInput").ap()
    for nm in ["wqT", "wkT", "wvT"]:
        dm[nm] = nc.dram_tensor(nm, [ND, D, HD], F32, kind="ExternalInput").ap()
    for nm in ["woT", "w0T"]:
        dm[nm] = nc.dram_tensor(nm, [D, D], F32, kind="ExternalInput").ap()
    dm["acT"] = nc.dram_tensor("acT", [D, HD], F32, kind="ExternalInput").ap()
    dm["btP"] = nc.dram_tensor("btP", [HD, D], F32, kind="ExternalInput").ap()
    dm["bqk"] = nc.dram_tensor("bqk", [HD, 96], F32, kind="ExternalInput").ap()
    dm["bo"] = nc.dram_tensor("bo", [1, D], F32, kind="ExternalInput").ap()
    dm["b0"] = nc.dram_tensor("b0", [1, D], F32, kind="ExternalInput").ap()
    dm["out"] = nc.dram_tensor("out", [T, D], F32, kind="ExternalOutput").ap()
    sk = "ExternalOutput" if debug else "Internal"
    dm["xn_scr"] = nc.dram_tensor("xn_scr", [ND, HD, T], BF16, kind=sk).ap()
    dm["q_scr"] = nc.dram_tensor("q_scr", [H, HD, T], BF16, kind=sk).ap()
    dm["k_scr"] = nc.dram_tensor("k_scr", [H, HD, T], BF16, kind=sk).ap()
    dm["v_scr"] = nc.dram_tensor("v_scr", [H, T, HD], BF16, kind=sk).ap()
    dm["ctx_scr"] = nc.dram_tensor("ctx_scr", [H, HD, T], BF16, kind=sk).ap()
    dm["s_scr"] = nc.dram_tensor("s_scr", [T, D], F32, kind=sk).ap()

    es = contextlib.ExitStack()
    with es:
        arena_t = es.enter_context(nc.sbuf_tensor("arena", [HD, ARENA_B], U8))
        arena = arena_t[:, :]

        def carve(lay, cols, dtype):
            nb = cols * {F32: 4, BF16: 2, U8: 1}[dtype]
            o = lay.take(nb)
            return arena[:, o:o + nb].bitcast(dtype)

        C = {}
        for name in ["TR", "TE", "TEA", "DV", "DV2", "DV3", "AC", "AC2",
                     "FX0", "FX1", "FW0", "FW1", "FQ0", "FQ1", "FB", "FM",
                     "FC0", "FC1", "SP0", "SP1", "SX", "SC0", "SC1",
                     "SS0", "SS1", "SO0", "SO1"]:
            C[name] = Ctr(es.enter_context(nc.semaphore(name)))
        TR = C["TR"]
        TE, TEA = C["TE"], C["TEA"]
        DV, DV2, DV3 = C["DV"], C["DV2"], C["DV3"]
        AC, AC2 = C["AC"], C["AC2"]
        FX = [C["FX0"], C["FX1"]]
        FW = [C["FW0"], C["FW1"]]
        FQ = [C["FQ0"], C["FQ1"]]
        FB, FM = C["FB"], C["FM"]
        FC = [C["FC0"], C["FC1"]]
        SP = [C["SP0"], C["SP1"]]
        SX = C["SX"]
        SC = [C["SC0"], C["SC1"]]
        SS = [C["SS0"], C["SS1"]]
        SO = [C["SO0"], C["SO1"]]
        ALL_DMA = [TR, FB, FM, SX] + FX + FW + FQ + FC + SP + SS + SO + SC

        p = Prog()

        per = Layout(0, ARENA_B)
        consts = carve(per, 16, F32)
        bqk_sb = carve(per, 96, F32)
        h_pad = carve(per, T, BF16)
        PERS_END = per.off

        p.raw("vector", lambda v: v.memset(consts[:, 0:1], EPS))
        p.raw("vector", lambda v: v.memset(consts[:, 1:2], -SHIFT))
        p.op("gpsimd", lambda g: g.dma_start(out=bqk_sb, in_=dm["bqk"]), FM, 16)
        p.wait("vector", FM)
        p.op("vector", lambda v: v.tensor_scalar(
            out=bqk_sb[:, 0:32], in0=bqk_sb[:, 0:32], scalar1=QSCALE,
            scalar2=None, op0=OP.mult), DV)

        psA = es.enter_context(nc.psum_tensor("psA", [HD, T], F32))
        psB = es.enter_context(nc.psum_tensor("psB", [HD, T], F32))

        l1 = Layout(PERS_END, ARENA_B)
        xn_fm = carve(l1, ND * T, BF16)
        xn_fm_v = xn_fm.rearrange("p (c q) -> p c q", q=T)
        l1a = Layout(l1.off, ARENA_B)
        xf = [carve(l1a, D, F32) for _ in range(2)]
        xnt = [carve(l1a, D, BF16) for _ in range(2)]
        bnb = carve(l1a, 48, F32)
        stat = carve(l1a, 8, F32)

        norm_done = {}
        tr_done1 = {}
        for t in range(NT):
            b = t % 2
            if t >= 2:
                p.wait("gpsimd", DV, norm_done[t - 2])
            p.op("gpsimd", lambda g, t=t, b=b: g.dma_start(
                out=xf[b], in_=dm["x"][t * 128:(t + 1) * 128, :]), FX[b], 16)
            p.wait("vector", FX[b])
            for c in range(8):
                p.op("vector", lambda v, b=b, c=c: v.bn_stats(
                    out=bnb[:, c * 6:(c + 1) * 6],
                    in_=xf[b][:, c * 512:(c + 1) * 512]), DV)
            p.wait("vector", DV)
            p.op("vector", lambda v: v.bn_aggr(out=stat[:, 0:2], in_=bnb), DV)
            p.wait("scalar", DV)
            p.op("scalar", lambda s: s.activation(
                out=stat[:, 2:3], in_=stat[:, 1:2], func=AF.Ln,
                bias=consts[:, 0:1]), AC)
            p.wait("scalar", AC)
            p.op("scalar", lambda s: s.activation(
                out=stat[:, 3:4], in_=stat[:, 2:3], func=AF.Exp, scale=-0.5), AC)
            p.wait("vector", AC)
            if t >= 2:
                p.wait("vector", TR, tr_done1[t - 2])
            p.op("vector", lambda v, b=b: v.tensor_scalar(
                out=xnt[b], in0=xf[b], scalar1=stat[:, 0:1],
                scalar2=stat[:, 3:4], op0=OP.subtract, op1=OP.mult), DV)
            norm_done[t] = DV.n
            p.wait("sync", DV)
            p.wait("sync", TR)
            p.op("sync", lambda s, t=t, b=b: s.dma_start(
                out=xn_fm_v[:, :, t * 128:(t + 1) * 128], in_=xnt[b],
                transpose=True), TR, trk)
            tr_done1[t] = TR.n
        p.wait("gpsimd", TR)
        for c in range(ND):
            p.op("gpsimd", lambda g, c=c: g.dma_start(
                out=dm["xn_scr"][c], in_=xn_fm[:, c * T:(c + 1) * T]), SX, 16)

        if phases < 1:
            return _finish(nc, p, ALL_DMA)
        l1b = Layout(l1.off, ARENA_B)
        wf = [carve(l1b, D, F32) for _ in range(2)]
        wb = [carve(l1b, D, BF16) for _ in range(2)]
        qsb = [carve(l1b, T, BF16) for _ in range(2)]
        vsb = carve(l1b, T, BF16)
        p.wait("gpsimd", TR)
        p.wait("gpsimd", SX)

        cast_done = {}
        pe_done = {}
        evac_done = {}
        store_done = {}
        vtrans_free = [None]

        for tag, wsrc, scol, sc in [("q", dm["wqT"], 0, QSCALE),
                                    ("k", dm["wkT"], 32, 1.0),
                                    ("v", dm["wvT"], 64, 1.0)]:
            if tag not in p1b_passes:
                continue
            scr = {"q": dm["q_scr"], "k": dm["k_scr"], "v": dm["v_scr"]}[tag]
            PREV = {"k": "q", "v": "k"}

            def pkey(d, tag, j):
                if (tag, j - 2) in d:
                    return d[(tag, j - 2)]
                pt = PREV.get(tag)
                if pt is not None and (pt, H - 2 + j) in d:
                    return d[(pt, H - 2 + j)]
                return None

            for j in range(H):
                b = j % 2
                pc = pkey(cast_done, tag, j)
                if pc is not None:
                    p.wait("gpsimd", DV, pc)
                p.op("gpsimd", lambda g, b=b, src=wsrc[j]: g.dma_start(
                    out=wf[b].rearrange("p (c d) -> p c d", d=HD),
                    in_=src.rearrange("(c p) d -> p c d", p=HD)), FW[b], 16)
                p.wait("vector", FW[b])
                pp = pkey(pe_done, tag, j)
                if pp is not None:
                    p.wait("vector", TE, pp)
                p.op("vector", lambda v, b=b: v.tensor_copy(
                    out=wb[b], in_=wf[b]), DV)
                cast_done[(tag, j)] = DV.n
                if qstop == "cast":
                    continue

                ps = psA if j % 2 == 0 else psB
                p.wait("tensor", DV, cast_done[(tag, j)])
                pe_ = pkey(evac_done, tag, j)
                if pe_ is not None:
                    p.wait("tensor", DV, pe_)
                wbv = wb[b].rearrange("p (c d) -> p c d", d=HD)

                def mmf(te, ps=ps, wbv=wbv):
                    last = None
                    for c in range(ND):
                        for n in range(4):
                            last = te.matmul(
                                ps[:, n * 512:(n + 1) * 512], wbv[:, c, :],
                                xn_fm[:, c * T + n * 512:c * T + (n + 1) * 512],
                                start=(c == 0), stop=(c == ND - 1))
                    return last
                p.op("tensor", mmf, TE)
                pe_done[(tag, j)] = TE.n
                if qstop == "mm":
                    continue
                p.wait("vector", TE)
                psd = pkey(store_done, tag, j)
                if psd is not None:
                    p.wait("vector", *psd)
                p.op("vector", lambda v, ps=ps, j=j, scol=scol, sc=sc:
                     v.tensor_scalar(out=qsb[j % 2], in0=ps[:, :], scalar1=sc,
                                     scalar2=bqk_sb[:, scol + j:scol + j + 1],
                                     op0=OP.mult, op1=OP.add), DV)
                evac_done[(tag, j)] = DV.n
                if qstop == "evac":
                    continue
                if tag in ("q", "k"):
                    p.wait("gpsimd", DV, evac_done[(tag, j)])
                    p.op("gpsimd", lambda g, scr=scr, j=j: g.dma_start(
                        out=scr[j], in_=qsb[j % 2]), SP[j % 2], 16)
                    store_done[(tag, j)] = (SP[j % 2], SP[j % 2].n)
                else:
                    p.wait("sync", DV, evac_done[(tag, j)])
                    if vtrans_free[0] is not None:
                        p.wait("sync", *vtrans_free[0])
                    p.wait("sync", TR)
                    p.op("sync", lambda s, j=j: s.dma_start(
                        out=vsb.rearrange("p (m b2) -> p m b2", b2=HD),
                        in_=qsb[j % 2], transpose=True), TR, trk)
                    p.wait("gpsimd", TR)
                    p.op("gpsimd", lambda g, j=j: g.dma_start(
                        out=dm["v_scr"][j].rearrange("(tt pp) d -> pp tt d",
                                                     pp=HD),
                        in_=vsb.rearrange("p (m b2) -> p m b2", b2=HD)),
                        SP[j % 2], 16)
                    store_done[(tag, j)] = (SP[j % 2], SP[j % 2].n)
                    vtrans_free[0] = (SP[j % 2], SP[j % 2].n)

        if "h" not in p1b_passes:
            return _finish(nc, p, ALL_DMA)
        p.wait("gpsimd", DV, cast_done[("v", H - 2)])
        p.op("gpsimd", lambda g: g.dma_start(
            out=wf[0].rearrange("p (c d) -> p c d", d=HD),
            in_=dm["acT"].rearrange("(c p) d -> p c d", p=HD)), FM, 16)
        p.wait("vector", FM)
        p.wait("vector", TE, pe_done[("v", H - 2)])
        p.op("vector", lambda v: v.tensor_copy(out=wb[0], in_=wf[0]), DV)
        h_cast = DV.n
        p.wait("tensor", DV, h_cast)
        p.wait("tensor", DV, evac_done[("v", H - 2)])
        wbv0 = wb[0].rearrange("p (c d) -> p c d", d=HD)

        def h_mm(te):
            last = None
            for c in range(ND):
                for n in range(4):
                    last = te.matmul(psA[:, n * 512:(n + 1) * 512], wbv0[:, c, :],
                                     xn_fm[:, c * T + n * 512:c * T + (n + 1) * 512],
                                     start=(c == 0), stop=(c == ND - 1))
            return last
        p.op("tensor", h_mm, TE)
        p.wait("vector", TE)
        p.op("vector", lambda v: v.tensor_copy(out=h_pad, in_=psA[:, :]), DV)

        p.barrier(ALL_DMA + [TE, DV, AC])

        if phases < 2:
            return _finish(nc, p, ALL_DMA)
        l2 = Layout(PERS_END, ARENA_B)
        junk = carve(l2, T, BF16)
        E_sb = carve(l2, NT * T, BF16)
        PT_sb = carve(l2, NT * T, BF16)
        PT_v = PT_sb.rearrange("p (j q) -> p j q", q=T)
        Pq = [carve(l2, T, BF16) for _ in range(2)]
        Pz = [carve(l2, T, BF16) for _ in range(2)]
        qh = [carve(l2, T, BF16) for _ in range(2)]
        kh = [carve(l2, T, BF16) for _ in range(2)]
        vh = [carve(l2, NT * HD, BF16) for _ in range(2)]
        ctxs = [carve(l2, T, BF16) for _ in range(2)]
        sm = carve(l2, 24 * NT, F32)

        def smcol(slot, t0, t1=None):
            t1 = t0 + 1 if t1 is None else t1
            return sm[:, slot * NT + t0:slot * NT + t1]

        qkv_ld = {}
        exp_done = {}
        mask_done = {}
        trans_done = {}
        avdone = {}
        ctx_evac = {}
        ctx_store = {}

        def emit_av(h):
            hb = h % 2
            p.wait("tensor", TR, trans_done[(h, NT - 1)])
            p.wait("tensor", AC, exp_done[(h, NT - 1)])
            p.wait("tensor", *qkv_ld[h])
            vhv = vh[hb].rearrange("p (m b2) -> p m b2", b2=HD)

            def av_mm(te, vhv=vhv):
                last = None
                for j in range(NT):
                    for n in range(4):
                        last = te.matmul(psB[:, n * 512:(n + 1) * 512],
                                         vhv[:, j, :],
                                         PT_v[:, j, n * 512:(n + 1) * 512],
                                         start=(j == 0), stop=(j == NT - 1))
                return last
            p.op("tensor", av_mm, TEA)
            avdone[h] = TEA.n
            p.wait("vector", TEA)
            if (h - 2) in ctx_store:
                p.wait("vector", SC[h % 2], ctx_store[h - 2])
            p.op("vector", lambda v, hb=hb: v.tensor_copy(
                out=ctxs[hb], in_=psB[:, :]), DV3)
            ctx_evac[h] = DV3.n
            p.wait("gpsimd", DV3, ctx_evac[h])
            p.op("gpsimd", lambda g, h=h, hb=hb: g.dma_start(
                out=dm["ctx_scr"][h], in_=ctxs[hb]), SC[h % 2], 16)
            ctx_store[h] = SC[h % 2].n

        for h in range(H):
            hb = h % 2
            if h >= 2:
                p.wait("gpsimd", TEA, avdone[h - 2])
            p.op("gpsimd", lambda g, h=h, hb=hb: g.dma_start(
                out=qh[hb], in_=dm["q_scr"][h]), FQ[hb], 16)
            p.op("gpsimd", lambda g, h=h, hb=hb: g.dma_start(
                out=kh[hb], in_=dm["k_scr"][h]), FQ[hb], 16)
            p.op("gpsimd", lambda g, h=h, hb=hb: g.dma_start(
                out=vh[hb].rearrange("p (m b2) -> p m b2", b2=HD),
                in_=dm["v_scr"][h].rearrange("(tt pp) d -> pp tt d", pp=HD)),
                FQ[hb], 16)
            qkv_ld[h] = (FQ[hb], FQ[hb].n)

            if h >= 1:
                emit_av(h - 1)

            for t in range(NT):
                ps = psA if t % 2 == 0 else psB
                p.wait("tensor", *qkv_ld[h])
                gidx = h * NT + t
                if gidx >= 2:
                    hp, tp = divmod(gidx - 2, NT)
                    p.wait("tensor", AC, exp_done[(hp, tp)])
                if t == 1 and h >= 1:
                    p.wait("tensor", DV3, ctx_evac[h - 1])

                def s_mm(te, ps=ps, hb=hb, t=t):
                    last = None
                    for n in range(4):
                        last = te.matmul(ps[:, n * 512:(n + 1) * 512],
                                         qh[hb][:, t * 128:(t + 1) * 128],
                                         kh[hb][:, n * 512:(n + 1) * 512],
                                         start=True, stop=True)
                    return last
                p.op("tensor", s_mm, TE)
                p.wait("scalar", TE)
                if h >= 1:
                    p.wait("scalar", DV3, mask_done[(h - 1, t)])
                p.op("scalar", lambda s, ps=ps, t=t: s.activation(
                    out=E_sb[:, t * T:(t + 1) * T], in_=ps[:, :], func=AF.Exp,
                    bias=consts[:, 1:2], accum_out=smcol(0, t)), AC)
                exp_done[(h, t)] = AC.n

            for grp in (list(range(0, 8)), list(range(8, 16))):
                lo, hi = grp[0], grp[-1] + 1

                def sc_(slot, lo=lo, hi=hi):
                    return smcol(slot, lo, hi)
                p.wait("scalar", AC, exp_done[(h, hi - 1)])
                p.op("scalar", lambda s, sc_=sc_: s.activation(
                    out=sc_(1), in_=sc_(0), func=AF.Ln), AC2)
                p.wait("vector", AC2)
                p.op("vector", lambda v, sc_=sc_: v.tensor_scalar(
                    out=sc_(1), in0=sc_(1), scalar1=C0, scalar2=None,
                    op0=OP.add), DV2)
                p.wait("scalar", DV2)
                p.op("scalar", lambda s, sc_=sc_: s.activation(
                    out=sc_(2), in_=sc_(1), func=AF.Exp,
                    bias=consts[:, 1:2]), AC2)
                p.wait("vector", AC2)
                p.op("vector", lambda v, sc_=sc_: v.tensor_scalar(
                    out=sc_(3), in0=sc_(2), scalar1=-1.0, scalar2=None,
                    op0=OP.mult), DV2)
                p.wait("scalar", DV2)
                for t in grp:
                    p.op("scalar", lambda s, t=t: s.activation(
                        out=junk, in_=E_sb[:, t * T:(t + 1) * T], func=AF.Sign,
                        bias=smcol(3, t), accum_out=smcol(4, t)), AC2)
                p.wait("vector", AC2)

                def dvchain(ops):
                    for f in ops:
                        p.op("vector", f, DV2)
                        p.wait("vector", DV2)

                chain1 = [
                    lambda v, sc_=sc_: v.tensor_scalar(
                        out=sc_(5), in0=sc_(4), scalar1=1.0 / 4096.0,
                        scalar2=0.5, op0=OP.mult, op1=OP.add),
                    lambda v, sc_=sc_: v.tensor_scalar(
                        out=sc_(5), in0=sc_(5), scalar1=0.01, scalar2=0.70,
                        op0=OP.max, op1=OP.min),
                    lambda v, sc_=sc_: v.tensor_scalar(
                        out=sc_(6), in0=sc_(5), scalar1=PC[0], scalar2=None,
                        op0=OP.mult),
                ]
                for ci in PC[1:-1]:
                    chain1.append(lambda v, sc_=sc_, ci=ci:
                                  v.scalar_tensor_tensor(
                                      out=sc_(6), in0=sc_(6), scalar=ci,
                                      in1=sc_(5), op0=OP.add, op1=OP.mult))
                chain1 += [
                    lambda v, sc_=sc_: v.tensor_scalar(
                        out=sc_(6), in0=sc_(6), scalar1=PC[-1], scalar2=None,
                        op0=OP.add),
                    lambda v, sc_=sc_: v.tensor_scalar(
                        out=sc_(7), in0=sc_(6), scalar1=-SIGMA,
                        scalar2=SIGMA * Z_P, op0=OP.mult, op1=OP.add),
                    lambda v, sc_=sc_: v.tensor_tensor(
                        out=sc_(8), in0=sc_(1), in1=sc_(7), op=OP.add),
                ]
                dvchain(chain1)
                p.wait("scalar", DV2)
                p.op("scalar", lambda s, sc_=sc_: s.activation(
                    out=sc_(9), in_=sc_(8), func=AF.Exp,
                    bias=consts[:, 1:2]), AC2)
                p.wait("vector", AC2)
                for t in grp:
                    p.op("vector", lambda v, t=t: v.tensor_scalar(
                        out=junk, in0=E_sb[:, t * T:(t + 1) * T],
                        scalar1=smcol(9, t), scalar2=None, op0=OP.is_ge,
                        op1=OP.add, accum_out=smcol(10, t)), DV2)
                    p.wait("vector", DV2)

                chain2 = [
                    lambda v, sc_=sc_: v.tensor_scalar(
                        out=sc_(11), in0=sc_(4), scalar1=0.5, scalar2=1024.0,
                        op0=OP.mult, op1=OP.add),
                    lambda v, sc_=sc_: v.tensor_tensor(
                        out=sc_(12), in0=sc_(10), in1=sc_(11), op=OP.subtract),
                    lambda v, sc_=sc_: v.scalar_tensor_tensor(
                        out=sc_(13), in0=sc_(12), scalar=0.0, in1=sc_(12),
                        op0=OP.add, op1=OP.mult),
                    lambda v, sc_=sc_: v.tensor_scalar(
                        out=sc_(13), in0=sc_(13), scalar1=0.25, scalar2=None,
                        op0=OP.add),
                    lambda v, sc_=sc_: v.reciprocal(out=sc_(14), in_=sc_(13)),
                    lambda v, sc_=sc_: v.tensor_scalar(
                        out=sc_(15), in0=sc_(10), scalar1=-1.0, scalar2=K_TOP,
                        op0=OP.mult, op1=OP.add),
                    lambda v, sc_=sc_: v.tensor_tensor(
                        out=sc_(16), in0=sc_(15), in1=sc_(7), op=OP.mult),
                    lambda v, sc_=sc_: v.tensor_tensor(
                        out=sc_(16), in0=sc_(16), in1=sc_(12), op=OP.mult),
                    lambda v, sc_=sc_: v.tensor_tensor(
                        out=sc_(16), in0=sc_(16), in1=sc_(14), op=OP.mult),
                    lambda v, sc_=sc_: v.tensor_scalar(
                        out=sc_(16), in0=sc_(16), scalar1=-1.0, scalar2=1.0,
                        op0=OP.max, op1=OP.min),
                    lambda v, sc_=sc_: v.tensor_tensor(
                        out=sc_(17), in0=sc_(8), in1=sc_(16), op=OP.add),
                ]
                dvchain(chain2)
                p.wait("scalar", DV2)
                p.op("scalar", lambda s, sc_=sc_: s.activation(
                    out=sc_(18), in_=sc_(17), func=AF.Exp,
                    bias=consts[:, 1:2]), AC2)
                p.wait("vector", AC2)

                for t in grp:
                    tb = t % 2
                    gidx = h * NT + t
                    if gidx >= 2:
                        hp, tp = divmod(gidx - 2, NT)
                        if (hp, tp) in trans_done:
                            p.wait("vector", TR, trans_done[(hp, tp)])
                    p.op("vector", lambda v, t=t, tb=tb:
                         v.scalar_tensor_tensor(
                             out=Pq[tb], in0=E_sb[:, t * T:(t + 1) * T],
                             scalar=smcol(18, t),
                             in1=E_sb[:, t * T:(t + 1) * T],
                             op0=OP.is_ge, op1=OP.mult,
                             accum_out=smcol(19, t)), DV3)
                    mask_done[(h, t)] = DV3.n
                    p.wait("vector", DV3)
                    p.op("vector", lambda v, t=t: v.reciprocal(
                        out=smcol(20, t), in_=smcol(19, t)), DV3)
                    p.wait("vector", DV3)
                    p.op("vector", lambda v, t=t, tb=tb: v.tensor_scalar(
                        out=Pz[tb], in0=Pq[tb], scalar1=smcol(20, t),
                        scalar2=None, op0=OP.mult), DV3)
                    p.wait("sync", DV3)
                    if h >= 1 and t == 0:
                        p.wait("sync", TEA, avdone[h - 1])
                    p.wait("sync", TR)
                    p.op("sync", lambda s, t=t, tb=tb: s.dma_start(
                        out=PT_v[:, :, t * 128:(t + 1) * 128], in_=Pz[tb],
                        transpose=True), TR, trk)
                    trans_done[(h, t)] = TR.n
        emit_av(H - 1)

        p.barrier(ALL_DMA + [TE, TEA, DV, DV2, DV3, AC, AC2])

        if phases < 2.5:
            return _finish(nc, p, ALL_DMA)
        l3 = Layout(PERS_END, ARENA_B)
        ctx3 = carve(l3, ND * T, BF16)
        ctx3v = ctx3.rearrange("p (c q) -> p c q", q=T)
        wo_bf = carve(l3, ND * 512, BF16)
        wo_bfv = wo_bf.rearrange("p (c n) -> p c n", n=512)
        wstg = [carve(l3, 512, F32) for _ in range(2)]
        bo_bc = carve(l3, 512, F32)
        s_sb = [carve(l3, 512, F32) for _ in range(2)]

        for c in range(ND):
            p.op("gpsimd", lambda g, c=c: g.dma_start(
                out=ctx3[:, c * T:(c + 1) * T], in_=dm["ctx_scr"][c]), FB, 16)
        p3a_ld = FB.n

        NOC = 8
        wcast = {}
        pe_p3 = {}
        sevac = {}
        sstore = {}
        for oc in range(NOC):
            ocs = oc * 512
            for c in range(ND):
                cb = c % 2
                if (oc, c - 2) in wcast:
                    p.wait("gpsimd", AC, wcast[(oc, c - 2)])
                p.op("gpsimd", lambda g, c=c, cb=cb, ocs=ocs: g.dma_start(
                    out=wstg[cb],
                    in_=dm["woT"][c * 128:(c + 1) * 128, ocs:ocs + 512]),
                    FW[cb], 16)
                p.wait("scalar", FW[cb])
                if oc > 0 and c == 0:
                    p.wait("scalar", TE, pe_p3[oc - 1])
                p.op("scalar", lambda s, c=c, cb=cb: s.activation(
                    out=wo_bfv[:, c, :], in_=wstg[cb], func=AF.Copy), AC)
                wcast[(oc, c)] = AC.n
            if oc > 0:
                p.wait("gpsimd", DV3, sevac[(oc - 1) * NT + NT - 1])
            p.op("gpsimd", lambda g, ocs=ocs: g.dma_start(
                out=bo_bc, in_=dm["bo"][0:1, ocs:ocs + 512]
                .broadcast_to([HD, 512])), FM, 16)
            bias_ld = FM.n

            for t in range(NT):
                tb = t % 2
                it = oc * NT + t
                p.wait("tensor", AC, wcast[(oc, ND - 1)])
                p.wait("tensor", FB, p3a_ld)
                if it >= 2:
                    p.wait("tensor", DV3, sevac[it - 2])

                def p3amm(te, t=t, tb=tb):
                    last = None
                    for c in range(ND):
                        last = te.matmul(psA[:, tb * 512:(tb + 1) * 512],
                                         ctx3v[:, c, t * 128:(t + 1) * 128],
                                         wo_bfv[:, c, :], start=(c == 0),
                                         stop=(c == ND - 1))
                    return last
                p.op("tensor", p3amm, TE)
                pe_p3[oc] = TE.n
                p.wait("vector", TE)
                p.wait("vector", FM, bias_ld)
                if it >= 2:
                    p.wait("vector", SS[it % 2], sstore[it - 2])
                p.op("vector", lambda v, tb=tb: v.tensor_tensor(
                    out=s_sb[tb], in0=psA[:, tb * 512:(tb + 1) * 512],
                    in1=bo_bc, op=OP.add), DV3)
                sevac[it] = DV3.n
                p.wait("gpsimd", DV3, sevac[it])
                p.op("gpsimd", lambda g, t=t, tb=tb, ocs=ocs: g.dma_start(
                    out=dm["s_scr"][t * 128:(t + 1) * 128, ocs:ocs + 512],
                    in_=s_sb[tb]), SS[it % 2], 16)
                sstore[it] = SS[it % 2].n

        p.barrier(ALL_DMA + [TE, DV3, AC])

        if phases < 3:
            return _finish(nc, p, ALL_DMA)
        l4 = Layout(PERS_END, ARENA_B)
        xn3 = carve(l4, ND * T, BF16)
        xn3v = xn3.rearrange("p (c q) -> p c q", q=T)
        w0_bf = carve(l4, ND * 512, BF16)
        w0_bfv = w0_bf.rearrange("p (c n) -> p c n", n=512)
        bt_bf = carve(l4, D, BF16)
        w0stg = [carve(l4, 512, F32) for _ in range(2)]
        selc = [carve(l4, 512, U8) for _ in range(2)]
        xc = [carve(l4, 512, F32) for _ in range(2)]
        sc3 = [carve(l4, 512, F32) for _ in range(2)]
        m1 = carve(l4, 512, U8)
        m2 = carve(l4, 512, U8)
        m3 = carve(l4, 512, U8)
        bl = carve(l4, 512, F32)
        Gc = carve(l4, 512, F32)
        gs = carve(l4, 512, F32)
        sgc = carve(l4, 512, F32)
        outc = [carve(l4, 512, F32) for _ in range(2)]
        b0_bc = carve(l4, 512, F32)

        bt_casts = {}
        for c in range(ND):
            p.op("gpsimd", lambda g, c=c: g.dma_start(
                out=xn3[:, c * T:(c + 1) * T], in_=dm["xn_scr"][c]), FB, 16)
        for cc in range(8):
            if cc >= 2:
                p.wait("gpsimd", DV, bt_casts[cc - 2])
            p.op("gpsimd", lambda g, cc=cc: g.dma_start(
                out=w0stg[cc % 2], in_=dm["btP"][:, cc * 512:(cc + 1) * 512]),
                FW[cc % 2], 16)
            p.wait("vector", FW[cc % 2])
            p.op("vector", lambda v, cc=cc: v.tensor_copy(
                out=bt_bf[:, cc * 512:(cc + 1) * 512], in_=w0stg[cc % 2]), DV)
            bt_casts[cc] = DV.n
        p3b_ld = FB.n
        bt_cast = DV.n

        ps_b = [psA[:, 0:512], psA[:, 512:1024]]
        ps_y = [psB[:, l * 512:(l + 1) * 512] for l in range(4)]

        wcast = {}
        pe_p3 = {}
        blend_done = {}
        outadd = {}
        ostore = {}
        sil_done = {}
        gready = {}

        for oc in range(NOC):
            ocs = oc * 512
            for c in range(ND):
                cb = c % 2
                if (oc, c - 2) in wcast:
                    p.wait("gpsimd", AC, wcast[(oc, c - 2)])
                if oc == 0 and c < 2:
                    p.wait("gpsimd", DV, bt_cast)
                p.op("gpsimd", lambda g, c=c, cb=cb, ocs=ocs: g.dma_start(
                    out=w0stg[cb],
                    in_=dm["w0T"][c * 128:(c + 1) * 128, ocs:ocs + 512]),
                    FW[cb], 16)
                p.wait("scalar", FW[cb])
                if oc > 0 and c == 0:
                    p.wait("scalar", TE, pe_p3[oc - 1])
                p.op("scalar", lambda s, c=c, cb=cb: s.activation(
                    out=w0_bfv[:, c, :], in_=w0stg[cb], func=AF.Copy), AC)
                wcast[(oc, c)] = AC.n
            if oc > 0:
                p.wait("gpsimd", DV3, outadd[(oc - 1) * NT + NT - 1])
            p.op("gpsimd", lambda g, ocs=ocs: g.dma_start(
                out=b0_bc, in_=dm["b0"][0:1, ocs:ocs + 512]
                .broadcast_to([HD, 512])), FM, 16)
            bias_ld = FM.n

            for t in range(NT):
                tb = t % 2
                it = oc * NT + t
                p.wait("tensor", AC, wcast[(oc, ND - 1)])
                p.wait("tensor", DV, bt_cast)
                p.wait("tensor", FB, p3b_ld)
                if it >= 2:
                    p.wait("tensor", DV3, gready[it - 2])

                def p3bmm(te, t=t, tb=tb):
                    last = None
                    for c in range(ND):
                        last = te.matmul(ps_b[tb],
                                         xn3v[:, c, t * 128:(t + 1) * 128],
                                         w0_bfv[:, c, :], start=(c == 0),
                                         stop=(c == ND - 1))
                    return last
                p.op("tensor", p3bmm, TE)
                if it >= 1:
                    p.wait("tensor", DV3, blend_done[it - 1])

                def p3y(te, t=t, ocs=ocs):
                    last = None
                    for l in range(4):
                        last = te.matmul(ps_y[l],
                                         h_pad[32 * l:32 * l + 16,
                                               t * 128:(t + 1) * 128],
                                         bt_bf[32 * l:32 * l + 16,
                                               ocs:ocs + 512],
                                         start=True, stop=True,
                                         tile_position=(32 * l, 0))
                    return last
                p.op("tensor", p3y, TE)
                pe_p3[oc] = TE.n
                pe_y = TE.n

                if it >= 2:
                    p.wait("gpsimd", DV3, blend_done[it - 2])
                    p.wait("gpsimd", DV3, outadd[it - 2])
                p.op("gpsimd", lambda g, t=t, tb=tb, ocs=ocs: g.dma_start(
                    out=selc[tb],
                    in_=dm["sel"][t * 128:(t + 1) * 128, ocs:ocs + 512]),
                    FC[tb], 16)
                p.op("gpsimd", lambda g, t=t, tb=tb, ocs=ocs: g.dma_start(
                    out=xc[tb],
                    in_=dm["x"][t * 128:(t + 1) * 128, ocs:ocs + 512]),
                    FC[tb], 16)
                p.op("gpsimd", lambda g, t=t, tb=tb, ocs=ocs: g.dma_start(
                    out=sc3[tb],
                    in_=dm["s_scr"][t * 128:(t + 1) * 128, ocs:ocs + 512]),
                    FC[tb], 16)
                ld_now = FC[tb].n

                p.wait("vector", FC[tb], ld_now)
                p.wait("vector", FM, bias_ld)
                p.wait("vector", TE, pe_y)
                if it >= 1:
                    p.wait("vector", AC2, sil_done[it - 1])
                for mm, lv in ((m1, 1.0), (m2, 2.0), (m3, 3.0)):
                    p.op("vector", lambda v, mm=mm, lv=lv, tb=tb:
                         v.tensor_scalar(out=mm, in0=selc[tb], scalar1=lv,
                                         scalar2=None, op0=OP.is_equal), DV3)
                    p.wait("vector", DV3)
                p.op("vector", lambda v: v.tensor_copy(
                    out=bl, in_=ps_y[0]), DV3)
                p.wait("vector", DV3)
                for mm, l in ((m1, 1), (m2, 2), (m3, 3)):
                    p.op("vector", lambda v, mm=mm, l=l: v.copy_predicated(
                        out=bl, mask=mm, data=ps_y[l]), DV3)
                    p.wait("vector", DV3)
                blend_done[it] = DV3.n
                p.op("vector", lambda v, tb=tb: v.scalar_tensor_tensor(
                    out=Gc, in0=bl, scalar=8.0, in1=ps_b[tb],
                    op0=OP.mult, op1=OP.add), DV3)
                p.wait("vector", DV3)
                p.op("vector", lambda v: v.tensor_tensor(
                    out=Gc, in0=Gc, in1=b0_bc, op=OP.add), DV3)
                g_ready = DV3.n
                gready[it] = g_ready
                p.wait("scalar", DV3, g_ready)
                p.op("scalar", lambda s: s.activation(
                    out=gs, in_=Gc, func=AF.Silu), AC2)
                sil_done[it] = AC2.n
                p.wait("vector", AC2, sil_done[it])
                if it >= 2:
                    p.wait("vector", SO[it % 2], ostore[it - 2])
                p.op("vector", lambda v, tb=tb: v.tensor_tensor(
                    out=sgc, in0=sc3[tb], in1=gs, op=OP.mult), DV3)
                p.wait("vector", DV3)
                if dump_g:
                    p.op("vector", lambda v, tb=tb: v.tensor_scalar(
                        out=outc[tb], in0=Gc, scalar1=1.0, scalar2=None,
                        op0=OP.mult), DV3)
                else:
                    p.op("vector", lambda v, tb=tb: v.tensor_tensor(
                        out=outc[tb], in0=sgc, in1=xc[tb], op=OP.add), DV3)
                outadd[it] = DV3.n
                p.wait("gpsimd", DV3, outadd[it])
                p.op("gpsimd", lambda g, t=t, tb=tb, ocs=ocs: g.dma_start(
                    out=dm["out"][t * 128:(t + 1) * 128, ocs:ocs + 512],
                    in_=outc[tb]), SO[it % 2], 16)
                ostore[it] = SO[it % 2].n

        return _finish(nc, p, ALL_DMA)


def _finish(nc, p, all_dma):
    if True:
        for c in all_dma:
            p.wait("gpsimd", c)

        with nc.Block() as block:
            @block.tensor
            def _(e):
                for fn in p.streams["tensor"]:
                    fn(e)

            @block.vector
            def _(e):
                for fn in p.streams["vector"]:
                    fn(e)

            @block.scalar
            def _(e):
                for fn in p.streams["scalar"]:
                    fn(e)

            @block.sync
            def _(e):
                for fn in p.streams["sync"]:
                    fn(e)

            @block.gpsimd
            def _(e):
                for fn in p.streams["gpsimd"]:
                    fn(e)
    return nc


def _prep_weights(inputs):
    f32c = lambda a: np.ascontiguousarray(np.asarray(a), dtype=np.float32)
    blocks = lambda w: np.ascontiguousarray(
        np.asarray(w, dtype=np.float32).reshape(ND, HD, D).transpose(0, 2, 1))
    d = {}
    d["wqT"] = blocks(inputs["L0_attn_wq"])
    d["wkT"] = blocks(inputs["L0_attn_wk"])
    d["wvT"] = blocks(inputs["L0_attn_wv"])
    d["woT"] = f32c(np.asarray(inputs["L0_attn_wo"]).T)
    d["w0T"] = f32c(np.asarray(inputs["L0_ffn_W0"]).T)
    A = np.asarray(inputs["L0_ffn_A"], dtype=np.float32)
    B = np.asarray(inputs["L0_ffn_B"], dtype=np.float32)
    acT = np.zeros((D, HD), np.float32)
    btP = np.zeros((HD, D), np.float32)
    for l in range(4):
        acT[:, 32 * l:32 * l + 16] = A[l].T
        btP[32 * l:32 * l + 16, :] = B[l].T
    d["acT"] = np.ascontiguousarray(acT)
    d["btP"] = np.ascontiguousarray(btP)
    bqk = np.zeros((HD, 96), np.float32)
    bqk[:, 0:32] = np.asarray(inputs["L0_attn_bq"], np.float32).reshape(ND, HD).T
    bqk[:, 32:64] = np.asarray(inputs["L0_attn_bk"], np.float32).reshape(ND, HD).T
    bqk[:, 64:96] = np.asarray(inputs["L0_attn_bv"], np.float32).reshape(ND, HD).T
    d["bqk"] = np.ascontiguousarray(bqk)
    d["bo"] = f32c(np.asarray(inputs["L0_attn_bo"]).reshape(1, D))
    d["b0"] = f32c(np.asarray(inputs["L0_ffn_b0"]).reshape(1, D))
    return d


def kernel(**inputs):
    x = np.ascontiguousarray(np.asarray(inputs["L0_attn_hidden_states"],
                                        dtype=np.float32))
    wd = _prep_weights(inputs)
    sel = make_sel()
    nc = build_program(debug=False)
    in_maps = []
    for b in range(8):
        m = dict(wd)
        m["x"] = np.ascontiguousarray(x[b])
        m["sel"] = np.ascontiguousarray(sel[b])
        in_maps.append(m)
    res = run_bass_kernel_spmd(nc, in_maps, core_ids=list(range(8)))
    return np.stack([res.results[b]["out"] for b in range(8)], axis=0)


# version 02
# speedup vs baseline: 1.0272x; 1.0272x over previous
import math
import contextlib
import numpy as np
import concourse.bass as bass
import concourse.mybir as mybir
from concourse.bass_utils import run_bass_kernel_spmd

F32 = mybir.dt.float32
BF16 = mybir.dt.bfloat16
U8 = mybir.dt.uint8
AF = mybir.ActivationFunctionType
OP = mybir.AluOpType

T = 2048
D = 4096
H = 32
HD = 128
NT = 16
ND = 32
NO2 = 16
K_TOP = 409.0
SHIFT = 16.0
SIGMA = 1.64
Z_P = 0.8416212335729143
C0 = SHIFT - math.log(2048.0) - SIGMA * SIGMA / 2 + Z_P * SIGMA
QSCALE = 1.0 / math.sqrt(128.0)
PC = [-111.33656853103689, 228.68416157385175, -180.57732498936406,
      69.0869892937092, -15.682627304359587, 2.317329186730592]
EPS = 1e-5
ARENA_B = 207872


def _rotl(x, d):
    return ((x << np.uint32(d)) | (x >> np.uint32(32 - d))).astype(np.uint32)


def _threefry2x32(k1, k2, x0, x1):
    ks0 = np.uint32(k1); ks1 = np.uint32(k2)
    ks2 = np.uint32(ks0 ^ ks1 ^ np.uint32(0x1BD11BDA))
    x0 = (x0 + ks0).astype(np.uint32); x1 = (x1 + ks1).astype(np.uint32)
    rot = [(13, 15, 26, 6), (17, 29, 16, 24)]
    keys = [(ks1, ks2), (ks2, ks0), (ks0, ks1), (ks1, ks2), (ks2, ks0)]
    for g in range(5):
        for r in rot[g % 2]:
            x0 = (x0 + x1).astype(np.uint32)
            x1 = _rotl(x1, r)
            x1 = (x1 ^ x0).astype(np.uint32)
        ka, kb = keys[g]
        x0 = (x0 + ka).astype(np.uint32)
        x1 = (x1 + kb + np.uint32(g + 1)).astype(np.uint32)
    return x0, x1


def make_sel():
    count = 8 * T * D
    i = np.arange(count, dtype=np.uint64)
    hi = (i >> np.uint64(32)).astype(np.uint32)
    lo = (i & np.uint64(0xFFFFFFFF)).astype(np.uint32)
    b1, b2 = _threefry2x32(np.uint32(0), np.uint32(42), hi, lo)
    return ((b1 ^ b2) >> np.uint32(30)).astype(np.uint8).reshape(8, T, D)


class Ctr:
    __slots__ = ("h", "n")

    def __init__(self, h):
        self.h = h
        self.n = 0


class Prog:
    ENGS = ("tensor", "vector", "scalar", "sync", "gpsimd")

    def __init__(self):
        self.streams = {e: [] for e in self.ENGS}

    def op(self, eng, fn, inc=None, k=1):
        if inc is None:
            self.streams[eng].append(fn)
            return
        inc.n += k
        kw = min(k, 16)
        if eng in ("vector", "scalar"):
            def w(e, fn=fn, h=inc.h, kw=kw):
                fn(e)
                e.drain().then_inc(h, kw)
            self.streams[eng].append(w)
        else:
            self.streams[eng].append(
                lambda e, fn=fn, h=inc.h, kw=kw: fn(e).then_inc(h, kw))

    def raw(self, eng, fn):
        self.streams[eng].append(fn)

    def wait(self, eng, ctr, val=None):
        v = ctr.n if val is None else val
        if v > 0:
            self.streams[eng].append(lambda e, h=ctr.h, v=v: e.wait_ge(h, v))

    def barrier(self, ctrs):
        for eng in self.ENGS:
            for c in ctrs:
                self.wait(eng, c)


class Layout:
    def __init__(self, base, limit):
        self.off = base
        self.limit = limit

    def take(self, nbytes, align=512):
        o = (self.off + align - 1) // align * align
        self.off = o + nbytes
        assert self.off <= self.limit, f"sbuf overflow {self.off} > {self.limit}"
        return o


def build_program(debug=False, phases=3, p1b_passes='qkvh', jmax=32, qstop='store', dump_g=False, trk=16):
    nc = bass.Bass("TRN2", target_bir_lowering=False)
    dm = {}
    dm["x"] = nc.dram_tensor("x", [T, D], F32, kind="ExternalInput").ap()
    dm["sel"] = nc.dram_tensor("sel", [T, D], U8, kind="ExternalInput").ap()
    for nm in ["wqT", "wkT", "wvT"]:
        dm[nm] = nc.dram_tensor(nm, [ND, HD, ND, HD], F32,
                                kind="ExternalInput").ap()
    for nm in ["woT", "w0T"]:
        dm[nm] = nc.dram_tensor(nm, [8, ND, HD, 512], F32,
                                kind="ExternalInput").ap()
    dm["acT"] = nc.dram_tensor("acT", [HD, ND, HD], F32,
                               kind="ExternalInput").ap()
    dm["btP"] = nc.dram_tensor("btP", [HD, D], F32, kind="ExternalInput").ap()
    dm["bqk"] = nc.dram_tensor("bqk", [HD, 96], F32, kind="ExternalInput").ap()
    dm["bo"] = nc.dram_tensor("bo", [1, D], F32, kind="ExternalInput").ap()
    dm["b0"] = nc.dram_tensor("b0", [1, D], F32, kind="ExternalInput").ap()
    dm["out"] = nc.dram_tensor("out", [T, D], F32, kind="ExternalOutput").ap()
    sk = "ExternalOutput" if debug else "Internal"
    dm["xn_scr"] = nc.dram_tensor("xn_scr", [ND, HD, T], BF16, kind=sk).ap()
    dm["q_scr"] = nc.dram_tensor("q_scr", [H, HD, T], BF16, kind=sk).ap()
    dm["k_scr"] = nc.dram_tensor("k_scr", [H, HD, T], BF16, kind=sk).ap()
    dm["v_scr"] = nc.dram_tensor("v_scr", [H, HD, NT * HD], BF16, kind=sk).ap()
    dm["ctx_scr"] = nc.dram_tensor("ctx_scr", [H, HD, T], BF16, kind=sk).ap()
    dm["s_scr"] = nc.dram_tensor("s_scr", [T, D], F32, kind=sk).ap()

    es = contextlib.ExitStack()
    with es:
        arena_t = es.enter_context(nc.sbuf_tensor("arena", [HD, ARENA_B], U8))
        arena = arena_t[:, :]

        def carve(lay, cols, dtype):
            nb = cols * {F32: 4, BF16: 2, U8: 1}[dtype]
            o = lay.take(nb)
            return arena[:, o:o + nb].bitcast(dtype)

        C = {}
        for name in ["TR", "TE", "TEA", "DV", "DV2", "DV3", "AC", "AC2",
                     "FX0", "FX1", "FW0", "FW1", "FQ0", "FQ1", "FB", "FM",
                     "FC0", "FC1", "SP0", "SP1", "SX", "SC0", "SC1",
                     "SS0", "SS1", "SO0", "SO1"]:
            C[name] = Ctr(es.enter_context(nc.semaphore(name)))
        TR = C["TR"]
        TE, TEA = C["TE"], C["TEA"]
        DV, DV2, DV3 = C["DV"], C["DV2"], C["DV3"]
        AC, AC2 = C["AC"], C["AC2"]
        FX = [C["FX0"], C["FX1"]]
        FW = [C["FW0"], C["FW1"]]
        FQ = [C["FQ0"], C["FQ1"]]
        FB, FM = C["FB"], C["FM"]
        FC = [C["FC0"], C["FC1"]]
        SP = [C["SP0"], C["SP1"]]
        SX = C["SX"]
        SC = [C["SC0"], C["SC1"]]
        SS = [C["SS0"], C["SS1"]]
        SO = [C["SO0"], C["SO1"]]
        ALL_DMA = [TR, FB, FM, SX] + FX + FW + FQ + FC + SP + SS + SO + SC

        p = Prog()

        per = Layout(0, ARENA_B)
        consts = carve(per, 16, F32)
        bqk_sb = carve(per, 96, F32)
        h_pad = carve(per, T, BF16)
        PERS_END = per.off

        p.raw("vector", lambda v: v.memset(consts[:, 0:1], EPS))
        p.raw("vector", lambda v: v.memset(consts[:, 1:2], -SHIFT))
        p.op("gpsimd", lambda g: g.dma_start(out=bqk_sb, in_=dm["bqk"]), FM, 16)
        p.wait("vector", FM)
        p.op("vector", lambda v: v.tensor_scalar(
            out=bqk_sb[:, 0:32], in0=bqk_sb[:, 0:32], scalar1=QSCALE,
            scalar2=None, op0=OP.mult), DV)

        psA = es.enter_context(nc.psum_tensor("psA", [HD, T], F32))
        psB = es.enter_context(nc.psum_tensor("psB", [HD, T], F32))

        l1 = Layout(PERS_END, ARENA_B)
        xn_fm = carve(l1, ND * T, BF16)
        xn_fm_v = xn_fm.rearrange("p (c q) -> p c q", q=T)
        l1a = Layout(l1.off, ARENA_B)
        xf = [carve(l1a, D, F32) for _ in range(2)]
        xnt = [carve(l1a, D, BF16) for _ in range(2)]
        bnb = carve(l1a, 48, F32)
        stat = carve(l1a, 8, F32)

        norm_done = {}
        tr_done1 = {}
        for t in range(NT):
            b = t % 2
            if t >= 2:
                p.wait("gpsimd", DV, norm_done[t - 2])
            p.op("gpsimd", lambda g, t=t, b=b: g.dma_start(
                out=xf[b], in_=dm["x"][t * 128:(t + 1) * 128, :]), FX[b], 16)
            p.wait("vector", FX[b])
            for c in range(8):
                p.op("vector", lambda v, b=b, c=c: v.bn_stats(
                    out=bnb[:, c * 6:(c + 1) * 6],
                    in_=xf[b][:, c * 512:(c + 1) * 512]), DV)
            p.op("vector", lambda v: v.bn_aggr(out=stat[:, 0:2], in_=bnb), DV)
            p.wait("scalar", DV)
            p.op("scalar", lambda s: s.activation(
                out=stat[:, 2:3], in_=stat[:, 1:2], func=AF.Ln,
                bias=consts[:, 0:1]), AC)
            p.op("scalar", lambda s: s.activation(
                out=stat[:, 3:4], in_=stat[:, 2:3], func=AF.Exp, scale=-0.5), AC)
            p.wait("vector", AC)
            if t >= 2:
                p.wait("vector", TR, tr_done1[t - 2])
            p.op("vector", lambda v, b=b: v.tensor_scalar(
                out=xnt[b], in0=xf[b], scalar1=stat[:, 0:1],
                scalar2=stat[:, 3:4], op0=OP.subtract, op1=OP.mult), DV)
            norm_done[t] = DV.n
            p.wait("sync", DV)
            p.wait("sync", TR)
            p.op("sync", lambda s, t=t, b=b: s.dma_start(
                out=xn_fm_v[:, :, t * 128:(t + 1) * 128], in_=xnt[b],
                transpose=True), TR, trk)
            tr_done1[t] = TR.n
        p.wait("gpsimd", TR)
        for c in range(ND):
            p.op("gpsimd", lambda g, c=c: g.dma_start(
                out=dm["xn_scr"][c], in_=xn_fm[:, c * T:(c + 1) * T]), SX, 16)

        if phases < 1:
            return _finish(nc, p, ALL_DMA)
        l1b = Layout(l1.off, ARENA_B)
        wf = [carve(l1b, D, F32) for _ in range(2)]
        wb = [carve(l1b, D, BF16) for _ in range(2)]
        qsb = [carve(l1b, T, BF16) for _ in range(2)]
        vsb = carve(l1b, T, BF16)
        p.wait("gpsimd", TR)
        p.wait("gpsimd", SX)

        cast_done = {}
        pe_done = {}
        evac_done = {}
        store_done = {}
        vtrans_free = [None]

        for tag, wsrc, scol, sc in [("q", dm["wqT"], 0, QSCALE),
                                    ("k", dm["wkT"], 32, 1.0),
                                    ("v", dm["wvT"], 64, 1.0)]:
            if tag not in p1b_passes:
                continue
            scr = {"q": dm["q_scr"], "k": dm["k_scr"], "v": dm["v_scr"]}[tag]
            PREV = {"k": "q", "v": "k"}

            def pkey(d, tag, j):
                if (tag, j - 2) in d:
                    return d[(tag, j - 2)]
                pt = PREV.get(tag)
                if pt is not None and (pt, H - 2 + j) in d:
                    return d[(pt, H - 2 + j)]
                return None

            for j in range(H):
                b = j % 2
                pc = pkey(cast_done, tag, j)
                if pc is not None:
                    p.wait("gpsimd", DV, pc)
                p.op("gpsimd", lambda g, b=b, src=wsrc[j]: g.dma_start(
                    out=wf[b].rearrange("p (c d) -> p c d", d=HD),
                    in_=src), FW[b], 16)
                p.wait("vector", FW[b])
                pp = pkey(pe_done, tag, j)
                if pp is not None:
                    p.wait("vector", TE, pp)
                p.op("vector", lambda v, b=b: v.tensor_copy(
                    out=wb[b], in_=wf[b]), DV)
                cast_done[(tag, j)] = DV.n
                if qstop == "cast":
                    continue

                ps = psA if j % 2 == 0 else psB
                p.wait("tensor", DV, cast_done[(tag, j)])
                pe_ = pkey(evac_done, tag, j)
                if pe_ is not None:
                    p.wait("tensor", DV, pe_)
                wbv = wb[b].rearrange("p (c d) -> p c d", d=HD)

                def mmf(te, ps=ps, wbv=wbv):
                    last = None
                    for c in range(ND):
                        for n in range(4):
                            last = te.matmul(
                                ps[:, n * 512:(n + 1) * 512], wbv[:, c, :],
                                xn_fm[:, c * T + n * 512:c * T + (n + 1) * 512],
                                start=(c == 0), stop=(c == ND - 1))
                    return last
                p.op("tensor", mmf, TE)
                pe_done[(tag, j)] = TE.n
                if qstop == "mm":
                    continue
                p.wait("vector", TE)
                psd = pkey(store_done, tag, j)
                if psd is not None:
                    p.wait("vector", *psd)
                p.op("vector", lambda v, ps=ps, j=j, scol=scol, sc=sc:
                     v.tensor_scalar(out=qsb[j % 2], in0=ps[:, :], scalar1=sc,
                                     scalar2=bqk_sb[:, scol + j:scol + j + 1],
                                     op0=OP.mult, op1=OP.add), DV)
                evac_done[(tag, j)] = DV.n
                if qstop == "evac":
                    continue
                if tag in ("q", "k"):
                    p.wait("gpsimd", DV, evac_done[(tag, j)])
                    p.op("gpsimd", lambda g, scr=scr, j=j: g.dma_start(
                        out=scr[j], in_=qsb[j % 2]), SP[j % 2], 16)
                    store_done[(tag, j)] = (SP[j % 2], SP[j % 2].n)
                else:
                    p.wait("sync", DV, evac_done[(tag, j)])
                    if vtrans_free[0] is not None:
                        p.wait("sync", *vtrans_free[0])
                    p.wait("sync", TR)
                    p.op("sync", lambda s, j=j: s.dma_start(
                        out=vsb.rearrange("p (m b2) -> p m b2", b2=HD),
                        in_=qsb[j % 2], transpose=True), TR, trk)
                    p.wait("gpsimd", TR)
                    p.op("gpsimd", lambda g, j=j: g.dma_start(
                        out=dm["v_scr"][j], in_=vsb), SP[j % 2], 16)
                    store_done[(tag, j)] = (SP[j % 2], SP[j % 2].n)
                    vtrans_free[0] = (SP[j % 2], SP[j % 2].n)

        if "h" not in p1b_passes:
            return _finish(nc, p, ALL_DMA)
        p.wait("gpsimd", DV, cast_done[("v", H - 2)])
        p.op("gpsimd", lambda g: g.dma_start(
            out=wf[0].rearrange("p (c d) -> p c d", d=HD),
            in_=dm["acT"]), FM, 16)
        p.wait("vector", FM)
        p.wait("vector", TE, pe_done[("v", H - 2)])
        p.op("vector", lambda v: v.tensor_copy(out=wb[0], in_=wf[0]), DV)
        h_cast = DV.n
        p.wait("tensor", DV, h_cast)
        p.wait("tensor", DV, evac_done[("v", H - 2)])
        wbv0 = wb[0].rearrange("p (c d) -> p c d", d=HD)

        def h_mm(te):
            last = None
            for c in range(ND):
                for n in range(4):
                    last = te.matmul(psA[:, n * 512:(n + 1) * 512], wbv0[:, c, :],
                                     xn_fm[:, c * T + n * 512:c * T + (n + 1) * 512],
                                     start=(c == 0), stop=(c == ND - 1))
            return last
        p.op("tensor", h_mm, TE)
        p.wait("vector", TE)
        p.op("vector", lambda v: v.tensor_copy(out=h_pad, in_=psA[:, :]), DV)

        p.barrier(ALL_DMA + [TE, DV, AC])

        if phases < 2:
            return _finish(nc, p, ALL_DMA)
        l2 = Layout(PERS_END, ARENA_B)
        junk = carve(l2, T, BF16)
        E_sb = carve(l2, NT * T, BF16)
        PT_sb = carve(l2, NT * T, BF16)
        PT_v = PT_sb.rearrange("p (j q) -> p j q", q=T)
        Pq = [carve(l2, T, BF16) for _ in range(2)]
        Pz = [carve(l2, T, BF16) for _ in range(2)]
        qh = [carve(l2, T, BF16) for _ in range(2)]
        kh = [carve(l2, T, BF16) for _ in range(2)]
        vh = [carve(l2, NT * HD, BF16) for _ in range(2)]
        ctxs = [carve(l2, T, BF16) for _ in range(2)]
        sm = carve(l2, 24 * NT, F32)

        def smcol(slot, t0, t1=None):
            t1 = t0 + 1 if t1 is None else t1
            return sm[:, slot * NT + t0:slot * NT + t1]

        qkv_ld = {}
        exp_done = {}
        mask_done = {}
        trans_done = {}
        avdone = {}
        ctx_evac = {}
        ctx_store = {}

        def emit_av(h):
            hb = h % 2
            p.wait("tensor", TR, trans_done[(h, NT - 1)])
            p.wait("tensor", AC, exp_done[(h, NT - 1)])
            p.wait("tensor", *qkv_ld[h])
            vhv = vh[hb].rearrange("p (m b2) -> p m b2", b2=HD)

            def av_mm(te, vhv=vhv):
                last = None
                for j in range(NT):
                    for n in range(4):
                        last = te.matmul(psB[:, n * 512:(n + 1) * 512],
                                         vhv[:, j, :],
                                         PT_v[:, j, n * 512:(n + 1) * 512],
                                         start=(j == 0), stop=(j == NT - 1))
                return last
            p.op("tensor", av_mm, TEA)
            avdone[h] = TEA.n
            p.wait("vector", TEA)
            if (h - 2) in ctx_store:
                p.wait("vector", SC[h % 2], ctx_store[h - 2])
            p.op("vector", lambda v, hb=hb: v.tensor_copy(
                out=ctxs[hb], in_=psB[:, :]), DV3)
            ctx_evac[h] = DV3.n
            p.wait("gpsimd", DV3, ctx_evac[h])
            p.op("gpsimd", lambda g, h=h, hb=hb: g.dma_start(
                out=dm["ctx_scr"][h], in_=ctxs[hb]), SC[h % 2], 16)
            ctx_store[h] = SC[h % 2].n

        for h in range(H):
            hb = h % 2
            if h >= 2:
                p.wait("gpsimd", TEA, avdone[h - 2])
            p.op("gpsimd", lambda g, h=h, hb=hb: g.dma_start(
                out=qh[hb], in_=dm["q_scr"][h]), FQ[hb], 16)
            p.op("gpsimd", lambda g, h=h, hb=hb: g.dma_start(
                out=kh[hb], in_=dm["k_scr"][h]), FQ[hb], 16)
            p.op("gpsimd", lambda g, h=h, hb=hb: g.dma_start(
                out=vh[hb], in_=dm["v_scr"][h]), FQ[hb], 16)
            qkv_ld[h] = (FQ[hb], FQ[hb].n)

            if h >= 1:
                emit_av(h - 1)

            for t in range(NT):
                ps = psA if t % 2 == 0 else psB
                p.wait("tensor", *qkv_ld[h])
                gidx = h * NT + t
                if gidx >= 2:
                    hp, tp = divmod(gidx - 2, NT)
                    p.wait("tensor", AC, exp_done[(hp, tp)])
                if t == 1 and h >= 1:
                    p.wait("tensor", DV3, ctx_evac[h - 1])

                def s_mm(te, ps=ps, hb=hb, t=t):
                    last = None
                    for n in range(4):
                        last = te.matmul(ps[:, n * 512:(n + 1) * 512],
                                         qh[hb][:, t * 128:(t + 1) * 128],
                                         kh[hb][:, n * 512:(n + 1) * 512],
                                         start=True, stop=True)
                    return last
                p.op("tensor", s_mm, TE)
                p.wait("scalar", TE)
                if h >= 1:
                    p.wait("scalar", DV3, mask_done[(h - 1, t)])
                p.op("scalar", lambda s, ps=ps, t=t: s.activation(
                    out=E_sb[:, t * T:(t + 1) * T], in_=ps[:, :], func=AF.Exp,
                    bias=consts[:, 1:2], accum_out=smcol(0, t)), AC)
                exp_done[(h, t)] = AC.n

            for grp in (list(range(0, 8)), list(range(8, 16))):
                lo, hi = grp[0], grp[-1] + 1

                def sc_(slot, lo=lo, hi=hi):
                    return smcol(slot, lo, hi)
                p.wait("scalar", AC, exp_done[(h, hi - 1)])
                p.op("scalar", lambda s, sc_=sc_: s.activation(
                    out=sc_(1), in_=sc_(0), func=AF.Ln), AC2)
                p.wait("vector", AC2)
                p.op("vector", lambda v, sc_=sc_: v.tensor_scalar(
                    out=sc_(1), in0=sc_(1), scalar1=C0, scalar2=None,
                    op0=OP.add), DV2)
                p.wait("scalar", DV2)
                p.op("scalar", lambda s, sc_=sc_: s.activation(
                    out=sc_(2), in_=sc_(1), func=AF.Exp,
                    bias=consts[:, 1:2]), AC2)
                p.wait("vector", AC2)
                p.op("vector", lambda v, sc_=sc_: v.tensor_scalar(
                    out=sc_(3), in0=sc_(2), scalar1=-1.0, scalar2=None,
                    op0=OP.mult), DV2)
                p.wait("scalar", DV2)
                for t in grp:
                    p.op("scalar", lambda s, t=t: s.activation(
                        out=junk, in_=E_sb[:, t * T:(t + 1) * T], func=AF.Sign,
                        bias=smcol(3, t), accum_out=smcol(4, t)), AC2)
                p.wait("vector", AC2)

                def dvchain(ops):
                    for f in ops:
                        p.op("vector", f, DV2)

                chain1 = [
                    lambda v, sc_=sc_: v.tensor_scalar(
                        out=sc_(5), in0=sc_(4), scalar1=1.0 / 4096.0,
                        scalar2=0.5, op0=OP.mult, op1=OP.add),
                    lambda v, sc_=sc_: v.tensor_scalar(
                        out=sc_(5), in0=sc_(5), scalar1=0.01, scalar2=0.70,
                        op0=OP.max, op1=OP.min),
                    lambda v, sc_=sc_: v.tensor_scalar(
                        out=sc_(6), in0=sc_(5), scalar1=PC[0], scalar2=None,
                        op0=OP.mult),
                ]
                for ci in PC[1:-1]:
                    chain1.append(lambda v, sc_=sc_, ci=ci:
                                  v.scalar_tensor_tensor(
                                      out=sc_(6), in0=sc_(6), scalar=ci,
                                      in1=sc_(5), op0=OP.add, op1=OP.mult))
                chain1 += [
                    lambda v, sc_=sc_: v.tensor_scalar(
                        out=sc_(6), in0=sc_(6), scalar1=PC[-1], scalar2=None,
                        op0=OP.add),
                    lambda v, sc_=sc_: v.tensor_scalar(
                        out=sc_(7), in0=sc_(6), scalar1=-SIGMA,
                        scalar2=SIGMA * Z_P, op0=OP.mult, op1=OP.add),
                    lambda v, sc_=sc_: v.tensor_tensor(
                        out=sc_(8), in0=sc_(1), in1=sc_(7), op=OP.add),
                ]
                dvchain(chain1)
                p.wait("scalar", DV2)
                p.op("scalar", lambda s, sc_=sc_: s.activation(
                    out=sc_(9), in_=sc_(8), func=AF.Exp,
                    bias=consts[:, 1:2]), AC2)
                p.wait("vector", AC2)
                for t in grp:
                    p.op("vector", lambda v, t=t: v.tensor_scalar(
                        out=junk, in0=E_sb[:, t * T:(t + 1) * T],
                        scalar1=smcol(9, t), scalar2=None, op0=OP.is_ge,
                        op1=OP.add, accum_out=smcol(10, t)), DV2)
                    p.wait("vector", DV2)

                chain2 = [
                    lambda v, sc_=sc_: v.tensor_scalar(
                        out=sc_(11), in0=sc_(4), scalar1=0.5, scalar2=1024.0,
                        op0=OP.mult, op1=OP.add),
                    lambda v, sc_=sc_: v.tensor_tensor(
                        out=sc_(12), in0=sc_(10), in1=sc_(11), op=OP.subtract),
                    lambda v, sc_=sc_: v.scalar_tensor_tensor(
                        out=sc_(13), in0=sc_(12), scalar=0.0, in1=sc_(12),
                        op0=OP.add, op1=OP.mult),
                    lambda v, sc_=sc_: v.tensor_scalar(
                        out=sc_(13), in0=sc_(13), scalar1=0.25, scalar2=None,
                        op0=OP.add),
                    lambda v, sc_=sc_: v.reciprocal(out=sc_(14), in_=sc_(13)),
                    lambda v, sc_=sc_: v.tensor_scalar(
                        out=sc_(15), in0=sc_(10), scalar1=-1.0, scalar2=K_TOP,
                        op0=OP.mult, op1=OP.add),
                    lambda v, sc_=sc_: v.tensor_tensor(
                        out=sc_(16), in0=sc_(15), in1=sc_(7), op=OP.mult),
                    lambda v, sc_=sc_: v.tensor_tensor(
                        out=sc_(16), in0=sc_(16), in1=sc_(12), op=OP.mult),
                    lambda v, sc_=sc_: v.tensor_tensor(
                        out=sc_(16), in0=sc_(16), in1=sc_(14), op=OP.mult),
                    lambda v, sc_=sc_: v.tensor_scalar(
                        out=sc_(16), in0=sc_(16), scalar1=-1.0, scalar2=1.0,
                        op0=OP.max, op1=OP.min),
                    lambda v, sc_=sc_: v.tensor_tensor(
                        out=sc_(17), in0=sc_(8), in1=sc_(16), op=OP.add),
                ]
                dvchain(chain2)
                p.wait("scalar", DV2)
                p.op("scalar", lambda s, sc_=sc_: s.activation(
                    out=sc_(18), in_=sc_(17), func=AF.Exp,
                    bias=consts[:, 1:2]), AC2)
                p.wait("vector", AC2)

                for t in grp:
                    tb = t % 2
                    gidx = h * NT + t
                    if gidx >= 2:
                        hp, tp = divmod(gidx - 2, NT)
                        if (hp, tp) in trans_done:
                            p.wait("vector", TR, trans_done[(hp, tp)])
                    p.op("vector", lambda v, t=t, tb=tb:
                         v.scalar_tensor_tensor(
                             out=Pq[tb], in0=E_sb[:, t * T:(t + 1) * T],
                             scalar=smcol(18, t),
                             in1=E_sb[:, t * T:(t + 1) * T],
                             op0=OP.is_ge, op1=OP.mult,
                             accum_out=smcol(19, t)), DV3)
                    mask_done[(h, t)] = DV3.n
                    p.op("vector", lambda v, t=t: v.reciprocal(
                        out=smcol(20, t), in_=smcol(19, t)), DV3)
                    p.op("vector", lambda v, t=t, tb=tb: v.tensor_scalar(
                        out=Pz[tb], in0=Pq[tb], scalar1=smcol(20, t),
                        scalar2=None, op0=OP.mult), DV3)
                    p.wait("sync", DV3)
                    if h >= 1 and t == 0:
                        p.wait("sync", TEA, avdone[h - 1])
                    p.wait("sync", TR)
                    p.op("sync", lambda s, t=t, tb=tb: s.dma_start(
                        out=PT_v[:, :, t * 128:(t + 1) * 128], in_=Pz[tb],
                        transpose=True), TR, trk)
                    trans_done[(h, t)] = TR.n
        emit_av(H - 1)

        p.barrier(ALL_DMA + [TE, TEA, DV, DV2, DV3, AC, AC2])

        if phases < 2.5:
            return _finish(nc, p, ALL_DMA)
        l3 = Layout(PERS_END, ARENA_B)
        ctx3 = carve(l3, ND * T, BF16)
        ctx3v = ctx3.rearrange("p (c q) -> p c q", q=T)
        wo_bf = carve(l3, ND * 512, BF16)
        wo_bfv = wo_bf.rearrange("p (c n) -> p c n", n=512)
        wstg = [carve(l3, 512, F32) for _ in range(2)]
        bo_bc = carve(l3, 512, F32)
        s_sb = [carve(l3, 512, F32) for _ in range(2)]

        for c in range(ND):
            p.op("gpsimd", lambda g, c=c: g.dma_start(
                out=ctx3[:, c * T:(c + 1) * T], in_=dm["ctx_scr"][c]), FB, 16)
        p3a_ld = FB.n

        NOC = 8
        wcast = {}
        pe_p3 = {}
        sevac = {}
        sstore = {}
        for oc in range(NOC):
            ocs = oc * 512
            for c in range(ND):
                cb = c % 2
                if (oc, c - 2) in wcast:
                    p.wait("gpsimd", AC, wcast[(oc, c - 2)])
                p.op("gpsimd", lambda g, c=c, cb=cb, oc=oc: g.dma_start(
                    out=wstg[cb], in_=dm["woT"][oc, c]), FW[cb], 16)
                p.wait("scalar", FW[cb])
                if oc > 0 and c == 0:
                    p.wait("scalar", TE, pe_p3[oc - 1])
                p.op("scalar", lambda s, c=c, cb=cb: s.activation(
                    out=wo_bfv[:, c, :], in_=wstg[cb], func=AF.Copy), AC)
                wcast[(oc, c)] = AC.n
            if oc > 0:
                p.wait("gpsimd", DV3, sevac[(oc - 1) * NT + NT - 1])
            p.op("gpsimd", lambda g, ocs=ocs: g.dma_start(
                out=bo_bc, in_=dm["bo"][0:1, ocs:ocs + 512]
                .broadcast_to([HD, 512])), FM, 16)
            bias_ld = FM.n

            for t in range(NT):
                tb = t % 2
                it = oc * NT + t
                p.wait("tensor", AC, wcast[(oc, ND - 1)])
                p.wait("tensor", FB, p3a_ld)
                if it >= 2:
                    p.wait("tensor", DV3, sevac[it - 2])

                def p3amm(te, t=t, tb=tb):
                    last = None
                    for c in range(ND):
                        last = te.matmul(psA[:, tb * 512:(tb + 1) * 512],
                                         ctx3v[:, c, t * 128:(t + 1) * 128],
                                         wo_bfv[:, c, :], start=(c == 0),
                                         stop=(c == ND - 1))
                    return last
                p.op("tensor", p3amm, TE)
                pe_p3[oc] = TE.n
                p.wait("vector", TE)
                p.wait("vector", FM, bias_ld)
                if it >= 2:
                    p.wait("vector", SS[it % 2], sstore[it - 2])
                p.op("vector", lambda v, tb=tb: v.tensor_tensor(
                    out=s_sb[tb], in0=psA[:, tb * 512:(tb + 1) * 512],
                    in1=bo_bc, op=OP.add), DV3)
                sevac[it] = DV3.n
                p.wait("gpsimd", DV3, sevac[it])
                p.op("gpsimd", lambda g, t=t, tb=tb, ocs=ocs: g.dma_start(
                    out=dm["s_scr"][t * 128:(t + 1) * 128, ocs:ocs + 512],
                    in_=s_sb[tb]), SS[it % 2], 16)
                sstore[it] = SS[it % 2].n

        p.barrier(ALL_DMA + [TE, DV3, AC])

        if phases < 3:
            return _finish(nc, p, ALL_DMA)
        l4 = Layout(PERS_END, ARENA_B)
        xn3 = carve(l4, ND * T, BF16)
        xn3v = xn3.rearrange("p (c q) -> p c q", q=T)
        w0_bf = carve(l4, ND * 512, BF16)
        w0_bfv = w0_bf.rearrange("p (c n) -> p c n", n=512)
        bt_bf = carve(l4, D, BF16)
        w0stg = [carve(l4, 512, F32) for _ in range(2)]
        selc = [carve(l4, 512, U8) for _ in range(2)]
        xc = [carve(l4, 512, F32) for _ in range(2)]
        sc3 = [carve(l4, 512, F32) for _ in range(2)]
        m1 = carve(l4, 512, U8)
        m2 = carve(l4, 512, U8)
        m3 = carve(l4, 512, U8)
        bl = carve(l4, 512, F32)
        Gc = carve(l4, 512, F32)
        gs = carve(l4, 512, F32)
        sgc = carve(l4, 512, F32)
        outc = [carve(l4, 512, F32) for _ in range(2)]
        b0_bc = carve(l4, 512, F32)

        bt_casts = {}
        for c in range(ND):
            p.op("gpsimd", lambda g, c=c: g.dma_start(
                out=xn3[:, c * T:(c + 1) * T], in_=dm["xn_scr"][c]), FB, 16)
        for cc in range(8):
            if cc >= 2:
                p.wait("gpsimd", DV, bt_casts[cc - 2])
            p.op("gpsimd", lambda g, cc=cc: g.dma_start(
                out=w0stg[cc % 2], in_=dm["btP"][:, cc * 512:(cc + 1) * 512]),
                FW[cc % 2], 16)
            p.wait("vector", FW[cc % 2])
            p.op("vector", lambda v, cc=cc: v.tensor_copy(
                out=bt_bf[:, cc * 512:(cc + 1) * 512], in_=w0stg[cc % 2]), DV)
            bt_casts[cc] = DV.n
        p3b_ld = FB.n
        bt_cast = DV.n

        ps_b = [psA[:, 0:512], psA[:, 512:1024]]
        ps_y = [psB[:, l * 512:(l + 1) * 512] for l in range(4)]

        wcast = {}
        pe_p3 = {}
        blend_done = {}
        outadd = {}
        ostore = {}
        sil_done = {}
        gready = {}

        for oc in range(NOC):
            ocs = oc * 512
            for c in range(ND):
                cb = c % 2
                if (oc, c - 2) in wcast:
                    p.wait("gpsimd", AC, wcast[(oc, c - 2)])
                if oc == 0 and c < 2:
                    p.wait("gpsimd", DV, bt_cast)
                p.op("gpsimd", lambda g, c=c, cb=cb, oc=oc: g.dma_start(
                    out=w0stg[cb], in_=dm["w0T"][oc, c]), FW[cb], 16)
                p.wait("scalar", FW[cb])
                if oc > 0 and c == 0:
                    p.wait("scalar", TE, pe_p3[oc - 1])
                p.op("scalar", lambda s, c=c, cb=cb: s.activation(
                    out=w0_bfv[:, c, :], in_=w0stg[cb], func=AF.Copy), AC)
                wcast[(oc, c)] = AC.n
            if oc > 0:
                p.wait("gpsimd", DV3, outadd[(oc - 1) * NT + NT - 1])
            p.op("gpsimd", lambda g, ocs=ocs: g.dma_start(
                out=b0_bc, in_=dm["b0"][0:1, ocs:ocs + 512]
                .broadcast_to([HD, 512])), FM, 16)
            bias_ld = FM.n

            for t in range(NT):
                tb = t % 2
                it = oc * NT + t
                p.wait("tensor", AC, wcast[(oc, ND - 1)])
                p.wait("tensor", DV, bt_cast)
                p.wait("tensor", FB, p3b_ld)
                if it >= 2:
                    p.wait("tensor", DV3, gready[it - 2])

                def p3bmm(te, t=t, tb=tb):
                    last = None
                    for c in range(ND):
                        last = te.matmul(ps_b[tb],
                                         xn3v[:, c, t * 128:(t + 1) * 128],
                                         w0_bfv[:, c, :], start=(c == 0),
                                         stop=(c == ND - 1))
                    return last
                p.op("tensor", p3bmm, TE)
                if it >= 1:
                    p.wait("tensor", DV3, blend_done[it - 1])

                def p3y(te, t=t, ocs=ocs):
                    last = None
                    for l in range(4):
                        last = te.matmul(ps_y[l],
                                         h_pad[32 * l:32 * l + 16,
                                               t * 128:(t + 1) * 128],
                                         bt_bf[32 * l:32 * l + 16,
                                               ocs:ocs + 512],
                                         start=True, stop=True,
                                         tile_position=(32 * l, 0))
                    return last
                p.op("tensor", p3y, TE)
                pe_p3[oc] = TE.n
                pe_y = TE.n

                if it >= 2:
                    p.wait("gpsimd", DV3, blend_done[it - 2])
                    p.wait("gpsimd", DV3, outadd[it - 2])
                p.op("gpsimd", lambda g, t=t, tb=tb, ocs=ocs: g.dma_start(
                    out=selc[tb],
                    in_=dm["sel"][t * 128:(t + 1) * 128, ocs:ocs + 512]),
                    FC[tb], 16)
                p.op("gpsimd", lambda g, t=t, tb=tb, ocs=ocs: g.dma_start(
                    out=xc[tb],
                    in_=dm["x"][t * 128:(t + 1) * 128, ocs:ocs + 512]),
                    FC[tb], 16)
                p.op("gpsimd", lambda g, t=t, tb=tb, ocs=ocs: g.dma_start(
                    out=sc3[tb],
                    in_=dm["s_scr"][t * 128:(t + 1) * 128, ocs:ocs + 512]),
                    FC[tb], 16)
                ld_now = FC[tb].n

                p.wait("vector", FC[tb], ld_now)
                p.wait("vector", FM, bias_ld)
                p.wait("vector", TE, pe_y)
                if it >= 1:
                    p.wait("vector", AC2, sil_done[it - 1])
                for mm, lv in ((m1, 1.0), (m2, 2.0), (m3, 3.0)):
                    p.op("vector", lambda v, mm=mm, lv=lv, tb=tb:
                         v.tensor_scalar(out=mm, in0=selc[tb], scalar1=lv,
                                         scalar2=None, op0=OP.is_equal), DV3)
                p.op("vector", lambda v: v.tensor_copy(
                    out=bl, in_=ps_y[0]), DV3)
                for mm, l in ((m1, 1), (m2, 2), (m3, 3)):
                    p.op("vector", lambda v, mm=mm, l=l: v.copy_predicated(
                        out=bl, mask=mm, data=ps_y[l]), DV3)
                blend_done[it] = DV3.n
                p.op("vector", lambda v, tb=tb: v.scalar_tensor_tensor(
                    out=Gc, in0=bl, scalar=8.0, in1=ps_b[tb],
                    op0=OP.mult, op1=OP.add), DV3)
                p.op("vector", lambda v: v.tensor_tensor(
                    out=Gc, in0=Gc, in1=b0_bc, op=OP.add), DV3)
                g_ready = DV3.n
                gready[it] = g_ready
                p.wait("scalar", DV3, g_ready)
                p.op("scalar", lambda s: s.activation(
                    out=gs, in_=Gc, func=AF.Silu), AC2)
                sil_done[it] = AC2.n
                p.wait("vector", AC2, sil_done[it])
                if it >= 2:
                    p.wait("vector", SO[it % 2], ostore[it - 2])
                p.op("vector", lambda v, tb=tb: v.tensor_tensor(
                    out=sgc, in0=sc3[tb], in1=gs, op=OP.mult), DV3)
                p.wait("vector", DV3)
                if dump_g:
                    p.op("vector", lambda v, tb=tb: v.tensor_scalar(
                        out=outc[tb], in0=Gc, scalar1=1.0, scalar2=None,
                        op0=OP.mult), DV3)
                else:
                    p.op("vector", lambda v, tb=tb: v.tensor_tensor(
                        out=outc[tb], in0=sgc, in1=xc[tb], op=OP.add), DV3)
                outadd[it] = DV3.n
                p.wait("gpsimd", DV3, outadd[it])
                p.op("gpsimd", lambda g, t=t, tb=tb, ocs=ocs: g.dma_start(
                    out=dm["out"][t * 128:(t + 1) * 128, ocs:ocs + 512],
                    in_=outc[tb]), SO[it % 2], 16)
                ostore[it] = SO[it % 2].n

        return _finish(nc, p, ALL_DMA)


def _finish(nc, p, all_dma):
    if True:
        for c in all_dma:
            p.wait("gpsimd", c)

        with nc.Block() as block:
            @block.tensor
            def _(e):
                for fn in p.streams["tensor"]:
                    fn(e)

            @block.vector
            def _(e):
                for fn in p.streams["vector"]:
                    fn(e)

            @block.scalar
            def _(e):
                for fn in p.streams["scalar"]:
                    fn(e)

            @block.sync
            def _(e):
                for fn in p.streams["sync"]:
                    fn(e)

            @block.gpsimd
            def _(e):
                for fn in p.streams["gpsimd"]:
                    fn(e)
    return nc


def _prep_weights(inputs):
    f32c = lambda a: np.ascontiguousarray(np.asarray(a), dtype=np.float32)
    def blocks(w):
        a = np.asarray(w, dtype=np.float32).reshape(ND, HD, ND, HD)
        return np.ascontiguousarray(a.transpose(0, 3, 2, 1))
    d = {}
    d["wqT"] = blocks(inputs["L0_attn_wq"])
    d["wkT"] = blocks(inputs["L0_attn_wk"])
    d["wvT"] = blocks(inputs["L0_attn_wv"])
    def ocstripes(w):
        a = np.asarray(w, dtype=np.float32).T.reshape(ND, HD, 8, 512)
        return np.ascontiguousarray(a.transpose(2, 0, 1, 3))
    d["woT"] = ocstripes(inputs["L0_attn_wo"])
    d["w0T"] = ocstripes(inputs["L0_ffn_W0"])
    A = np.asarray(inputs["L0_ffn_A"], dtype=np.float32)
    B = np.asarray(inputs["L0_ffn_B"], dtype=np.float32)
    acT = np.zeros((D, HD), np.float32)
    btP = np.zeros((HD, D), np.float32)
    for l in range(4):
        acT[:, 32 * l:32 * l + 16] = A[l].T
        btP[32 * l:32 * l + 16, :] = B[l].T
    d["acT"] = np.ascontiguousarray(
        acT.reshape(ND, HD, HD).transpose(1, 0, 2))
    d["btP"] = np.ascontiguousarray(btP)
    bqk = np.zeros((HD, 96), np.float32)
    bqk[:, 0:32] = np.asarray(inputs["L0_attn_bq"], np.float32).reshape(ND, HD).T
    bqk[:, 32:64] = np.asarray(inputs["L0_attn_bk"], np.float32).reshape(ND, HD).T
    bqk[:, 64:96] = np.asarray(inputs["L0_attn_bv"], np.float32).reshape(ND, HD).T
    d["bqk"] = np.ascontiguousarray(bqk)
    d["bo"] = f32c(np.asarray(inputs["L0_attn_bo"]).reshape(1, D))
    d["b0"] = f32c(np.asarray(inputs["L0_ffn_b0"]).reshape(1, D))
    return d


def kernel(**inputs):
    x = np.ascontiguousarray(np.asarray(inputs["L0_attn_hidden_states"],
                                        dtype=np.float32))
    wd = _prep_weights(inputs)
    sel = make_sel()
    nc = build_program(debug=False)
    in_maps = []
    for b in range(8):
        m = dict(wd)
        m["x"] = np.ascontiguousarray(x[b])
        m["sel"] = np.ascontiguousarray(sel[b])
        in_maps.append(m)
    res = run_bass_kernel_spmd(nc, in_maps, core_ids=list(range(8)))
    return np.stack([res.results[b]["out"] for b in range(8)], axis=0)


# version 03 (final)
# speedup vs baseline: 1.0500x; 1.0222x over previous
import math
import contextlib
import numpy as np
import concourse.bass as bass
import concourse.mybir as mybir
from concourse.bass_utils import run_bass_kernel_spmd

F32 = mybir.dt.float32
BF16 = mybir.dt.bfloat16
U8 = mybir.dt.uint8
AF = mybir.ActivationFunctionType
OP = mybir.AluOpType

T = 2048
D = 4096
H = 32
HD = 128
NT = 16
ND = 32
NO2 = 16
K_TOP = 409.0
SHIFT = 16.0
SIGMA = 1.64
Z_P = 0.8416212335729143
C0 = SHIFT - math.log(2048.0) - SIGMA * SIGMA / 2 + Z_P * SIGMA
QSCALE = 1.0 / math.sqrt(128.0)
PC = [-111.33656853103689, 228.68416157385175, -180.57732498936406,
      69.0869892937092, -15.682627304359587, 2.317329186730592]
EPS = 1e-5
ARENA_B = 207872


def _rotl(x, d):
    return ((x << np.uint32(d)) | (x >> np.uint32(32 - d))).astype(np.uint32)


def _threefry2x32(k1, k2, x0, x1):
    ks0 = np.uint32(k1); ks1 = np.uint32(k2)
    ks2 = np.uint32(ks0 ^ ks1 ^ np.uint32(0x1BD11BDA))
    x0 = (x0 + ks0).astype(np.uint32); x1 = (x1 + ks1).astype(np.uint32)
    rot = [(13, 15, 26, 6), (17, 29, 16, 24)]
    keys = [(ks1, ks2), (ks2, ks0), (ks0, ks1), (ks1, ks2), (ks2, ks0)]
    for g in range(5):
        for r in rot[g % 2]:
            x0 = (x0 + x1).astype(np.uint32)
            x1 = _rotl(x1, r)
            x1 = (x1 ^ x0).astype(np.uint32)
        ka, kb = keys[g]
        x0 = (x0 + ka).astype(np.uint32)
        x1 = (x1 + kb + np.uint32(g + 1)).astype(np.uint32)
    return x0, x1


def make_sel():
    count = 8 * T * D
    i = np.arange(count, dtype=np.uint64)
    hi = (i >> np.uint64(32)).astype(np.uint32)
    lo = (i & np.uint64(0xFFFFFFFF)).astype(np.uint32)
    b1, b2 = _threefry2x32(np.uint32(0), np.uint32(42), hi, lo)
    return ((b1 ^ b2) >> np.uint32(30)).astype(np.uint8).reshape(8, T, D)


class Ctr:
    __slots__ = ("h", "n")

    def __init__(self, h):
        self.h = h
        self.n = 0


class Prog:
    ENGS = ("tensor", "vector", "scalar", "sync", "gpsimd")

    def __init__(self):
        self.streams = {e: [] for e in self.ENGS}

    def op(self, eng, fn, inc=None, k=1):
        if inc is None:
            self.streams[eng].append(fn)
            return
        inc.n += k
        kw = min(k, 16)
        if eng in ("vector", "scalar"):
            def w(e, fn=fn, h=inc.h, kw=kw):
                fn(e)
                e.drain().then_inc(h, kw)
            self.streams[eng].append(w)
        else:
            self.streams[eng].append(
                lambda e, fn=fn, h=inc.h, kw=kw: fn(e).then_inc(h, kw))

    def raw(self, eng, fn):
        self.streams[eng].append(fn)

    def wait(self, eng, ctr, val=None):
        v = ctr.n if val is None else val
        if v > 0:
            self.streams[eng].append(lambda e, h=ctr.h, v=v: e.wait_ge(h, v))

    def barrier(self, ctrs):
        for eng in self.ENGS:
            for c in ctrs:
                self.wait(eng, c)


class Layout:
    def __init__(self, base, limit):
        self.off = base
        self.limit = limit

    def take(self, nbytes, align=512):
        o = (self.off + align - 1) // align * align
        self.off = o + nbytes
        assert self.off <= self.limit, f"sbuf overflow {self.off} > {self.limit}"
        return o


def build_program(debug=False, phases=3, p1b_passes='qkvh', jmax=32, qstop='store', dump_g=False, trk=16, ser_tr=False):
    nc = bass.Bass("TRN2", target_bir_lowering=False)
    dm = {}
    dm["x"] = nc.dram_tensor("x", [T, D], F32, kind="ExternalInput").ap()
    dm["sel"] = nc.dram_tensor("sel", [T, D], U8, kind="ExternalInput").ap()
    for nm in ["wqT", "wkT", "wvT"]:
        dm[nm] = nc.dram_tensor(nm, [ND, HD, ND, HD], F32,
                                kind="ExternalInput").ap()
    for nm in ["woT", "w0T"]:
        dm[nm] = nc.dram_tensor(nm, [8, ND, HD, 512], F32,
                                kind="ExternalInput").ap()
    dm["acT"] = nc.dram_tensor("acT", [HD, ND, HD], F32,
                               kind="ExternalInput").ap()
    dm["btP"] = nc.dram_tensor("btP", [HD, D], F32, kind="ExternalInput").ap()
    dm["bqk"] = nc.dram_tensor("bqk", [HD, 96], F32, kind="ExternalInput").ap()
    dm["bo"] = nc.dram_tensor("bo", [1, D], F32, kind="ExternalInput").ap()
    dm["b0"] = nc.dram_tensor("b0", [1, D], F32, kind="ExternalInput").ap()
    dm["out"] = nc.dram_tensor("out", [T, D], F32, kind="ExternalOutput").ap()
    sk = "ExternalOutput" if debug else "Internal"
    dm["xn_scr"] = nc.dram_tensor("xn_scr", [ND, HD, T], BF16, kind=sk).ap()
    dm["q_scr"] = nc.dram_tensor("q_scr", [H, HD, T], BF16, kind=sk).ap()
    dm["k_scr"] = nc.dram_tensor("k_scr", [H, HD, T], BF16, kind=sk).ap()
    dm["v_scr"] = nc.dram_tensor("v_scr", [H, HD, NT * HD], BF16, kind=sk).ap()
    dm["ctx_scr"] = nc.dram_tensor("ctx_scr", [H, HD, T], BF16, kind=sk).ap()
    dm["s_scr"] = nc.dram_tensor("s_scr", [T, D], F32, kind=sk).ap()

    es = contextlib.ExitStack()
    with es:
        arena_t = es.enter_context(nc.sbuf_tensor("arena", [HD, ARENA_B], U8))
        arena = arena_t[:, :]

        def carve(lay, cols, dtype):
            nb = cols * {F32: 4, BF16: 2, U8: 1}[dtype]
            o = lay.take(nb)
            return arena[:, o:o + nb].bitcast(dtype)

        C = {}
        for name in ["TR", "TE", "TEA", "DV", "DV2", "DV3", "AC", "AC2",
                     "FX0", "FX1", "FW0", "FW1", "FQ0", "FQ1", "FB", "FM",
                     "FC0", "FC1", "SP0", "SP1", "SX", "SC0", "SC1",
                     "SS0", "SS1", "SO0", "SO1"]:
            C[name] = Ctr(es.enter_context(nc.semaphore(name)))
        TR = C["TR"]
        TE, TEA = C["TE"], C["TEA"]
        DV, DV2, DV3 = C["DV"], C["DV2"], C["DV3"]
        AC, AC2 = C["AC"], C["AC2"]
        FX = [C["FX0"], C["FX1"]]
        FW = [C["FW0"], C["FW1"]]
        FQ = [C["FQ0"], C["FQ1"]]
        FB, FM = C["FB"], C["FM"]
        FC = [C["FC0"], C["FC1"]]
        SP = [C["SP0"], C["SP1"]]
        SX = C["SX"]
        SC = [C["SC0"], C["SC1"]]
        SS = [C["SS0"], C["SS1"]]
        SO = [C["SO0"], C["SO1"]]
        ALL_DMA = [TR, FB, FM, SX] + FX + FW + FQ + FC + SP + SS + SO + SC

        p = Prog()

        per = Layout(0, ARENA_B)
        consts = carve(per, 16, F32)
        bqk_sb = carve(per, 96, F32)
        h_pad = carve(per, T, BF16)
        PERS_END = per.off

        p.raw("vector", lambda v: v.memset(consts[:, 0:1], EPS))
        p.raw("vector", lambda v: v.memset(consts[:, 1:2], -SHIFT))
        p.op("gpsimd", lambda g: g.dma_start(out=bqk_sb, in_=dm["bqk"]), FM, 16)
        p.wait("vector", FM)
        p.op("vector", lambda v: v.tensor_scalar(
            out=bqk_sb[:, 0:32], in0=bqk_sb[:, 0:32], scalar1=QSCALE,
            scalar2=None, op0=OP.mult), DV)

        psA = es.enter_context(nc.psum_tensor("psA", [HD, T], F32))
        psB = es.enter_context(nc.psum_tensor("psB", [HD, T], F32))

        l1 = Layout(PERS_END, ARENA_B)
        xn_fm = carve(l1, ND * T, BF16)
        xn_fm_v = xn_fm.rearrange("p (c q) -> p c q", q=T)
        l1a = Layout(l1.off, ARENA_B)
        xf = [carve(l1a, D, F32) for _ in range(2)]
        xnt = [carve(l1a, D, BF16) for _ in range(2)]
        bnb = carve(l1a, 48, F32)
        stat = carve(l1a, 8, F32)

        norm_done = {}
        tr_done1 = {}
        for t in range(NT):
            b = t % 2
            if t >= 2:
                p.wait("gpsimd", DV, norm_done[t - 2])
            p.op("gpsimd", lambda g, t=t, b=b: g.dma_start(
                out=xf[b], in_=dm["x"][t * 128:(t + 1) * 128, :]), FX[b], 16)
            p.wait("vector", FX[b])
            for c in range(8):
                p.op("vector", lambda v, b=b, c=c: v.bn_stats(
                    out=bnb[:, c * 6:(c + 1) * 6],
                    in_=xf[b][:, c * 512:(c + 1) * 512]), DV)
            p.op("vector", lambda v: v.bn_aggr(out=stat[:, 0:2], in_=bnb), DV)
            p.wait("scalar", DV)
            p.op("scalar", lambda s: s.activation(
                out=stat[:, 2:3], in_=stat[:, 1:2], func=AF.Ln,
                bias=consts[:, 0:1]), AC)
            p.op("scalar", lambda s: s.activation(
                out=stat[:, 3:4], in_=stat[:, 2:3], func=AF.Exp, scale=-0.5), AC)
            p.wait("vector", AC)
            if t >= 2:
                p.wait("vector", TR, tr_done1[t - 2])
            p.op("vector", lambda v, b=b: v.tensor_scalar(
                out=xnt[b], in0=xf[b], scalar1=stat[:, 0:1],
                scalar2=stat[:, 3:4], op0=OP.subtract, op1=OP.mult), DV)
            norm_done[t] = DV.n
            p.wait("sync", DV)
            if ser_tr:
                p.wait("sync", TR)
            p.op("sync", lambda s, t=t, b=b: s.dma_start(
                out=xn_fm_v[:, :, t * 128:(t + 1) * 128], in_=xnt[b],
                transpose=True), TR, trk)
            tr_done1[t] = TR.n
        p.wait("gpsimd", TR)
        for c in range(ND):
            p.op("gpsimd", lambda g, c=c: g.dma_start(
                out=dm["xn_scr"][c], in_=xn_fm[:, c * T:(c + 1) * T]), SX, 16)

        if phases < 1:
            return _finish(nc, p, ALL_DMA)
        l1b = Layout(l1.off, ARENA_B)
        wf = [carve(l1b, D, F32) for _ in range(2)]
        wb = [carve(l1b, D, BF16) for _ in range(2)]
        qsb = [carve(l1b, T, BF16) for _ in range(2)]
        vsb = carve(l1b, T, BF16)
        p.wait("gpsimd", TR)
        p.wait("gpsimd", SX)

        cast_done = {}
        pe_done = {}
        evac_done = {}
        store_done = {}
        vtrans_free = [None]

        for tag, wsrc, scol, sc in [("q", dm["wqT"], 0, QSCALE),
                                    ("k", dm["wkT"], 32, 1.0),
                                    ("v", dm["wvT"], 64, 1.0)]:
            if tag not in p1b_passes:
                continue
            scr = {"q": dm["q_scr"], "k": dm["k_scr"], "v": dm["v_scr"]}[tag]
            PREV = {"k": "q", "v": "k"}

            def pkey(d, tag, j):
                if (tag, j - 2) in d:
                    return d[(tag, j - 2)]
                pt = PREV.get(tag)
                if pt is not None and (pt, H - 2 + j) in d:
                    return d[(pt, H - 2 + j)]
                return None

            for j in range(H):
                b = j % 2
                pc = pkey(cast_done, tag, j)
                if pc is not None:
                    p.wait("gpsimd", DV, pc)
                p.op("gpsimd", lambda g, b=b, src=wsrc[j]: g.dma_start(
                    out=wf[b].rearrange("p (c d) -> p c d", d=HD),
                    in_=src), FW[b], 16)
                p.wait("vector", FW[b])
                pp = pkey(pe_done, tag, j)
                if pp is not None:
                    p.wait("vector", TE, pp)
                p.op("vector", lambda v, b=b: v.tensor_copy(
                    out=wb[b], in_=wf[b]), DV)
                cast_done[(tag, j)] = DV.n
                if qstop == "cast":
                    continue

                ps = psA if j % 2 == 0 else psB
                p.wait("tensor", DV, cast_done[(tag, j)])
                pe_ = pkey(evac_done, tag, j)
                if pe_ is not None:
                    p.wait("tensor", DV, pe_)
                wbv = wb[b].rearrange("p (c d) -> p c d", d=HD)

                def mmf(te, ps=ps, wbv=wbv):
                    last = None
                    for c in range(ND):
                        for n in range(4):
                            last = te.matmul(
                                ps[:, n * 512:(n + 1) * 512], wbv[:, c, :],
                                xn_fm[:, c * T + n * 512:c * T + (n + 1) * 512],
                                start=(c == 0), stop=(c == ND - 1))
                    return last
                p.op("tensor", mmf, TE)
                pe_done[(tag, j)] = TE.n
                if qstop == "mm":
                    continue
                p.wait("vector", TE)
                psd = pkey(store_done, tag, j)
                if psd is not None:
                    p.wait("vector", *psd)
                p.op("vector", lambda v, ps=ps, j=j, scol=scol, sc=sc:
                     v.tensor_scalar(out=qsb[j % 2], in0=ps[:, :], scalar1=sc,
                                     scalar2=bqk_sb[:, scol + j:scol + j + 1],
                                     op0=OP.mult, op1=OP.add), DV)
                evac_done[(tag, j)] = DV.n
                if qstop == "evac":
                    continue
                if tag in ("q", "k"):
                    p.wait("gpsimd", DV, evac_done[(tag, j)])
                    p.op("gpsimd", lambda g, scr=scr, j=j: g.dma_start(
                        out=scr[j], in_=qsb[j % 2]), SP[j % 2], 16)
                    store_done[(tag, j)] = (SP[j % 2], SP[j % 2].n)
                else:
                    p.wait("sync", DV, evac_done[(tag, j)])
                    if vtrans_free[0] is not None:
                        p.wait("sync", *vtrans_free[0])
                    if ser_tr:
                        p.wait("sync", TR)
                    p.op("sync", lambda s, j=j: s.dma_start(
                        out=vsb.rearrange("p (m b2) -> p m b2", b2=HD),
                        in_=qsb[j % 2], transpose=True), TR, trk)
                    p.wait("gpsimd", TR)
                    p.op("gpsimd", lambda g, j=j: g.dma_start(
                        out=dm["v_scr"][j], in_=vsb), SP[j % 2], 16)
                    store_done[(tag, j)] = (SP[j % 2], SP[j % 2].n)
                    vtrans_free[0] = (SP[j % 2], SP[j % 2].n)

        if "h" not in p1b_passes:
            return _finish(nc, p, ALL_DMA)
        p.wait("gpsimd", DV, cast_done[("v", H - 2)])
        p.op("gpsimd", lambda g: g.dma_start(
            out=wf[0].rearrange("p (c d) -> p c d", d=HD),
            in_=dm["acT"]), FM, 16)
        p.wait("vector", FM)
        p.wait("vector", TE, pe_done[("v", H - 2)])
        p.op("vector", lambda v: v.tensor_copy(out=wb[0], in_=wf[0]), DV)
        h_cast = DV.n
        p.wait("tensor", DV, h_cast)
        p.wait("tensor", DV, evac_done[("v", H - 2)])
        wbv0 = wb[0].rearrange("p (c d) -> p c d", d=HD)

        def h_mm(te):
            last = None
            for c in range(ND):
                for n in range(4):
                    last = te.matmul(psA[:, n * 512:(n + 1) * 512], wbv0[:, c, :],
                                     xn_fm[:, c * T + n * 512:c * T + (n + 1) * 512],
                                     start=(c == 0), stop=(c == ND - 1))
            return last
        p.op("tensor", h_mm, TE)
        p.wait("vector", TE)
        p.op("vector", lambda v: v.tensor_copy(out=h_pad, in_=psA[:, :]), DV)

        p.barrier(ALL_DMA + [TE, DV, AC])

        if phases < 2:
            return _finish(nc, p, ALL_DMA)
        l2 = Layout(PERS_END, ARENA_B)
        junk = carve(l2, T, BF16)
        E_sb = carve(l2, NT * T, BF16)
        PT_sb = carve(l2, NT * T, BF16)
        PT_v = PT_sb.rearrange("p (j q) -> p j q", q=T)
        Pq = [carve(l2, T, BF16) for _ in range(2)]
        Pz = [carve(l2, T, BF16) for _ in range(2)]
        qh = [carve(l2, T, BF16) for _ in range(2)]
        kh = [carve(l2, T, BF16) for _ in range(2)]
        vh = [carve(l2, NT * HD, BF16) for _ in range(2)]
        ctxs = [carve(l2, T, BF16) for _ in range(2)]
        sm = carve(l2, 24 * NT, F32)

        def smcol(slot, t0, t1=None):
            t1 = t0 + 1 if t1 is None else t1
            return sm[:, slot * NT + t0:slot * NT + t1]

        qkv_ld = {}
        exp_done = {}
        mask_done = {}
        trans_done = {}
        avdone = {}
        ctx_evac = {}
        ctx_store = {}

        def emit_av(h):
            hb = h % 2
            p.wait("tensor", TR, trans_done[(h, NT - 1)])
            p.wait("tensor", AC, exp_done[(h, NT - 1)])
            p.wait("tensor", *qkv_ld[h])
            vhv = vh[hb].rearrange("p (m b2) -> p m b2", b2=HD)

            def av_mm(te, vhv=vhv):
                last = None
                for j in range(NT):
                    for n in range(4):
                        last = te.matmul(psB[:, n * 512:(n + 1) * 512],
                                         vhv[:, j, :],
                                         PT_v[:, j, n * 512:(n + 1) * 512],
                                         start=(j == 0), stop=(j == NT - 1))
                return last
            p.op("tensor", av_mm, TEA)
            avdone[h] = TEA.n
            p.wait("vector", TEA)
            if (h - 2) in ctx_store:
                p.wait("vector", SC[h % 2], ctx_store[h - 2])
            p.op("vector", lambda v, hb=hb: v.tensor_copy(
                out=ctxs[hb], in_=psB[:, :]), DV3)
            ctx_evac[h] = DV3.n
            p.wait("gpsimd", DV3, ctx_evac[h])
            p.op("gpsimd", lambda g, h=h, hb=hb: g.dma_start(
                out=dm["ctx_scr"][h], in_=ctxs[hb]), SC[h % 2], 16)
            ctx_store[h] = SC[h % 2].n

        for h in range(H):
            hb = h % 2
            if h >= 2:
                p.wait("gpsimd", TEA, avdone[h - 2])
            p.op("gpsimd", lambda g, h=h, hb=hb: g.dma_start(
                out=qh[hb], in_=dm["q_scr"][h]), FQ[hb], 16)
            p.op("gpsimd", lambda g, h=h, hb=hb: g.dma_start(
                out=kh[hb], in_=dm["k_scr"][h]), FQ[hb], 16)
            p.op("gpsimd", lambda g, h=h, hb=hb: g.dma_start(
                out=vh[hb], in_=dm["v_scr"][h]), FQ[hb], 16)
            qkv_ld[h] = (FQ[hb], FQ[hb].n)

            if h >= 1:
                emit_av(h - 1)

            for t in range(NT):
                ps = psA if t % 2 == 0 else psB
                p.wait("tensor", *qkv_ld[h])
                gidx = h * NT + t
                if gidx >= 2:
                    hp, tp = divmod(gidx - 2, NT)
                    p.wait("tensor", AC, exp_done[(hp, tp)])
                if t == 1 and h >= 1:
                    p.wait("tensor", DV3, ctx_evac[h - 1])

                def s_mm(te, ps=ps, hb=hb, t=t):
                    last = None
                    for n in range(4):
                        last = te.matmul(ps[:, n * 512:(n + 1) * 512],
                                         qh[hb][:, t * 128:(t + 1) * 128],
                                         kh[hb][:, n * 512:(n + 1) * 512],
                                         start=True, stop=True)
                    return last
                p.op("tensor", s_mm, TE)
                p.wait("scalar", TE)
                if h >= 1:
                    p.wait("scalar", DV3, mask_done[(h - 1, t)])
                p.op("scalar", lambda s, ps=ps, t=t: s.activation(
                    out=E_sb[:, t * T:(t + 1) * T], in_=ps[:, :], func=AF.Exp,
                    bias=consts[:, 1:2], accum_out=smcol(0, t)), AC)
                exp_done[(h, t)] = AC.n

            for grp in (list(range(0, 8)), list(range(8, 16))):
                lo, hi = grp[0], grp[-1] + 1

                def sc_(slot, lo=lo, hi=hi):
                    return smcol(slot, lo, hi)
                p.wait("scalar", AC, exp_done[(h, hi - 1)])
                p.op("scalar", lambda s, sc_=sc_: s.activation(
                    out=sc_(1), in_=sc_(0), func=AF.Ln), AC2)
                p.wait("vector", AC2)
                p.op("vector", lambda v, sc_=sc_: v.tensor_scalar(
                    out=sc_(1), in0=sc_(1), scalar1=C0, scalar2=None,
                    op0=OP.add), DV2)
                p.wait("scalar", DV2)
                p.op("scalar", lambda s, sc_=sc_: s.activation(
                    out=sc_(2), in_=sc_(1), func=AF.Exp,
                    bias=consts[:, 1:2]), AC2)
                p.wait("vector", AC2)
                p.op("vector", lambda v, sc_=sc_: v.tensor_scalar(
                    out=sc_(3), in0=sc_(2), scalar1=-1.0, scalar2=None,
                    op0=OP.mult), DV2)
                p.wait("scalar", DV2)
                for t in grp:
                    p.op("scalar", lambda s, t=t: s.activation(
                        out=junk, in_=E_sb[:, t * T:(t + 1) * T], func=AF.Sign,
                        bias=smcol(3, t), accum_out=smcol(4, t)), AC2)
                p.wait("vector", AC2)

                def dvchain(ops):
                    for f in ops:
                        p.op("vector", f, DV2)

                chain1 = [
                    lambda v, sc_=sc_: v.tensor_scalar(
                        out=sc_(5), in0=sc_(4), scalar1=1.0 / 4096.0,
                        scalar2=0.5, op0=OP.mult, op1=OP.add),
                    lambda v, sc_=sc_: v.tensor_scalar(
                        out=sc_(5), in0=sc_(5), scalar1=0.01, scalar2=0.70,
                        op0=OP.max, op1=OP.min),
                    lambda v, sc_=sc_: v.tensor_scalar(
                        out=sc_(6), in0=sc_(5), scalar1=PC[0], scalar2=None,
                        op0=OP.mult),
                ]
                for ci in PC[1:-1]:
                    chain1.append(lambda v, sc_=sc_, ci=ci:
                                  v.scalar_tensor_tensor(
                                      out=sc_(6), in0=sc_(6), scalar=ci,
                                      in1=sc_(5), op0=OP.add, op1=OP.mult))
                chain1 += [
                    lambda v, sc_=sc_: v.tensor_scalar(
                        out=sc_(6), in0=sc_(6), scalar1=PC[-1], scalar2=None,
                        op0=OP.add),
                    lambda v, sc_=sc_: v.tensor_scalar(
                        out=sc_(7), in0=sc_(6), scalar1=-SIGMA,
                        scalar2=SIGMA * Z_P, op0=OP.mult, op1=OP.add),
                    lambda v, sc_=sc_: v.tensor_tensor(
                        out=sc_(8), in0=sc_(1), in1=sc_(7), op=OP.add),
                ]
                dvchain(chain1)
                p.wait("scalar", DV2)
                p.op("scalar", lambda s, sc_=sc_: s.activation(
                    out=sc_(9), in_=sc_(8), func=AF.Exp,
                    bias=consts[:, 1:2]), AC2)
                p.wait("vector", AC2)
                for t in grp:
                    p.op("vector", lambda v, t=t: v.tensor_scalar(
                        out=junk, in0=E_sb[:, t * T:(t + 1) * T],
                        scalar1=smcol(9, t), scalar2=None, op0=OP.is_ge,
                        op1=OP.add, accum_out=smcol(10, t)), DV2)
                    p.wait("vector", DV2)

                chain2 = [
                    lambda v, sc_=sc_: v.tensor_scalar(
                        out=sc_(11), in0=sc_(4), scalar1=0.5, scalar2=1024.0,
                        op0=OP.mult, op1=OP.add),
                    lambda v, sc_=sc_: v.tensor_tensor(
                        out=sc_(12), in0=sc_(10), in1=sc_(11), op=OP.subtract),
                    lambda v, sc_=sc_: v.scalar_tensor_tensor(
                        out=sc_(13), in0=sc_(12), scalar=0.0, in1=sc_(12),
                        op0=OP.add, op1=OP.mult),
                    lambda v, sc_=sc_: v.tensor_scalar(
                        out=sc_(13), in0=sc_(13), scalar1=0.25, scalar2=None,
                        op0=OP.add),
                    lambda v, sc_=sc_: v.reciprocal(out=sc_(14), in_=sc_(13)),
                    lambda v, sc_=sc_: v.tensor_scalar(
                        out=sc_(15), in0=sc_(10), scalar1=-1.0, scalar2=K_TOP,
                        op0=OP.mult, op1=OP.add),
                    lambda v, sc_=sc_: v.tensor_tensor(
                        out=sc_(16), in0=sc_(15), in1=sc_(7), op=OP.mult),
                    lambda v, sc_=sc_: v.tensor_tensor(
                        out=sc_(16), in0=sc_(16), in1=sc_(12), op=OP.mult),
                    lambda v, sc_=sc_: v.tensor_tensor(
                        out=sc_(16), in0=sc_(16), in1=sc_(14), op=OP.mult),
                    lambda v, sc_=sc_: v.tensor_scalar(
                        out=sc_(16), in0=sc_(16), scalar1=-1.0, scalar2=1.0,
                        op0=OP.max, op1=OP.min),
                    lambda v, sc_=sc_: v.tensor_tensor(
                        out=sc_(17), in0=sc_(8), in1=sc_(16), op=OP.add),
                ]
                dvchain(chain2)
                p.wait("scalar", DV2)
                p.op("scalar", lambda s, sc_=sc_: s.activation(
                    out=sc_(18), in_=sc_(17), func=AF.Exp,
                    bias=consts[:, 1:2]), AC2)
                p.wait("vector", AC2)

                for t in grp:
                    tb = t % 2
                    gidx = h * NT + t
                    if gidx >= 2:
                        hp, tp = divmod(gidx - 2, NT)
                        if (hp, tp) in trans_done:
                            p.wait("vector", TR, trans_done[(hp, tp)])
                    p.op("vector", lambda v, t=t, tb=tb:
                         v.scalar_tensor_tensor(
                             out=Pq[tb], in0=E_sb[:, t * T:(t + 1) * T],
                             scalar=smcol(18, t),
                             in1=E_sb[:, t * T:(t + 1) * T],
                             op0=OP.is_ge, op1=OP.mult,
                             accum_out=smcol(19, t)), DV3)
                    mask_done[(h, t)] = DV3.n
                    p.op("vector", lambda v, t=t: v.reciprocal(
                        out=smcol(20, t), in_=smcol(19, t)), DV3)
                    p.op("vector", lambda v, t=t, tb=tb: v.tensor_scalar(
                        out=Pz[tb], in0=Pq[tb], scalar1=smcol(20, t),
                        scalar2=None, op0=OP.mult), DV3)
                    p.wait("sync", DV3)
                    if h >= 1 and t == 0:
                        p.wait("sync", TEA, avdone[h - 1])
                    if ser_tr:
                        p.wait("sync", TR)
                    p.op("sync", lambda s, t=t, tb=tb: s.dma_start(
                        out=PT_v[:, :, t * 128:(t + 1) * 128], in_=Pz[tb],
                        transpose=True), TR, trk)
                    trans_done[(h, t)] = TR.n
        emit_av(H - 1)

        p.barrier(ALL_DMA + [TE, TEA, DV, DV2, DV3, AC, AC2])

        if phases < 2.5:
            return _finish(nc, p, ALL_DMA)
        l3 = Layout(PERS_END, ARENA_B)
        ctx3 = carve(l3, ND * T, BF16)
        ctx3v = ctx3.rearrange("p (c q) -> p c q", q=T)
        wo_bf = carve(l3, ND * 512, BF16)
        wo_bfv = wo_bf.rearrange("p (c n) -> p c n", n=512)
        wstg = [carve(l3, 512, F32) for _ in range(2)]
        bo_bc = carve(l3, 512, F32)
        s_sb = [carve(l3, 512, F32) for _ in range(2)]

        for c in range(ND):
            p.op("gpsimd", lambda g, c=c: g.dma_start(
                out=ctx3[:, c * T:(c + 1) * T], in_=dm["ctx_scr"][c]), FB, 16)
        p3a_ld = FB.n

        NOC = 8
        wcast = {}
        pe_p3 = {}
        sevac = {}
        sstore = {}
        for oc in range(NOC):
            ocs = oc * 512
            for c in range(ND):
                cb = c % 2
                if (oc, c - 2) in wcast:
                    p.wait("gpsimd", AC, wcast[(oc, c - 2)])
                p.op("gpsimd", lambda g, c=c, cb=cb, oc=oc: g.dma_start(
                    out=wstg[cb], in_=dm["woT"][oc, c]), FW[cb], 16)
                p.wait("scalar", FW[cb])
                if oc > 0 and c == 0:
                    p.wait("scalar", TE, pe_p3[oc - 1])
                p.op("scalar", lambda s, c=c, cb=cb: s.activation(
                    out=wo_bfv[:, c, :], in_=wstg[cb], func=AF.Copy), AC)
                wcast[(oc, c)] = AC.n
            if oc > 0:
                p.wait("gpsimd", DV3, sevac[(oc - 1) * NT + NT - 1])
            p.op("gpsimd", lambda g, ocs=ocs: g.dma_start(
                out=bo_bc, in_=dm["bo"][0:1, ocs:ocs + 512]
                .broadcast_to([HD, 512])), FM, 16)
            bias_ld = FM.n

            for t in range(NT):
                tb = t % 2
                it = oc * NT + t
                p.wait("tensor", AC, wcast[(oc, ND - 1)])
                p.wait("tensor", FB, p3a_ld)
                if it >= 2:
                    p.wait("tensor", DV3, sevac[it - 2])

                def p3amm(te, t=t, tb=tb):
                    last = None
                    for c in range(ND):
                        last = te.matmul(psA[:, tb * 512:(tb + 1) * 512],
                                         ctx3v[:, c, t * 128:(t + 1) * 128],
                                         wo_bfv[:, c, :], start=(c == 0),
                                         stop=(c == ND - 1))
                    return last
                p.op("tensor", p3amm, TE)
                pe_p3[oc] = TE.n
                p.wait("vector", TE)
                p.wait("vector", FM, bias_ld)
                if it >= 2:
                    p.wait("vector", SS[it % 2], sstore[it - 2])
                p.op("vector", lambda v, tb=tb: v.tensor_tensor(
                    out=s_sb[tb], in0=psA[:, tb * 512:(tb + 1) * 512],
                    in1=bo_bc, op=OP.add), DV3)
                sevac[it] = DV3.n
                p.wait("gpsimd", DV3, sevac[it])
                p.op("gpsimd", lambda g, t=t, tb=tb, ocs=ocs: g.dma_start(
                    out=dm["s_scr"][t * 128:(t + 1) * 128, ocs:ocs + 512],
                    in_=s_sb[tb]), SS[it % 2], 16)
                sstore[it] = SS[it % 2].n

        p.barrier(ALL_DMA + [TE, DV3, AC])

        if phases < 3:
            return _finish(nc, p, ALL_DMA)
        l4 = Layout(PERS_END, ARENA_B)
        xn3 = carve(l4, ND * T, BF16)
        xn3v = xn3.rearrange("p (c q) -> p c q", q=T)
        w0_bf = carve(l4, ND * 512, BF16)
        w0_bfv = w0_bf.rearrange("p (c n) -> p c n", n=512)
        bt_bf = carve(l4, D, BF16)
        w0stg = [carve(l4, 512, F32) for _ in range(2)]
        selc = [carve(l4, 512, U8) for _ in range(2)]
        xc = [carve(l4, 512, F32) for _ in range(2)]
        sc3 = [carve(l4, 512, F32) for _ in range(2)]
        m1 = carve(l4, 512, U8)
        m2 = carve(l4, 512, U8)
        m3 = carve(l4, 512, U8)
        bl = carve(l4, 512, F32)
        Gc = carve(l4, 512, F32)
        gs = carve(l4, 512, F32)
        sgc = carve(l4, 512, F32)
        outc = [carve(l4, 512, F32) for _ in range(2)]
        b0_bc = carve(l4, 512, F32)

        bt_casts = {}
        for c in range(ND):
            p.op("gpsimd", lambda g, c=c: g.dma_start(
                out=xn3[:, c * T:(c + 1) * T], in_=dm["xn_scr"][c]), FB, 16)
        for cc in range(8):
            if cc >= 2:
                p.wait("gpsimd", DV, bt_casts[cc - 2])
            p.op("gpsimd", lambda g, cc=cc: g.dma_start(
                out=w0stg[cc % 2], in_=dm["btP"][:, cc * 512:(cc + 1) * 512]),
                FW[cc % 2], 16)
            p.wait("vector", FW[cc % 2])
            p.op("vector", lambda v, cc=cc: v.tensor_copy(
                out=bt_bf[:, cc * 512:(cc + 1) * 512], in_=w0stg[cc % 2]), DV)
            bt_casts[cc] = DV.n
        p3b_ld = FB.n
        bt_cast = DV.n

        ps_b = [psA[:, 0:512], psA[:, 512:1024]]
        ps_y = [psB[:, l * 512:(l + 1) * 512] for l in range(4)]

        wcast = {}
        pe_p3 = {}
        blend_done = {}
        outadd = {}
        ostore = {}
        sil_done = {}
        gready = {}

        for oc in range(NOC):
            ocs = oc * 512
            for c in range(ND):
                cb = c % 2
                if (oc, c - 2) in wcast:
                    p.wait("gpsimd", AC, wcast[(oc, c - 2)])
                if oc == 0 and c < 2:
                    p.wait("gpsimd", DV, bt_cast)
                p.op("gpsimd", lambda g, c=c, cb=cb, oc=oc: g.dma_start(
                    out=w0stg[cb], in_=dm["w0T"][oc, c]), FW[cb], 16)
                p.wait("scalar", FW[cb])
                if oc > 0 and c == 0:
                    p.wait("scalar", TE, pe_p3[oc - 1])
                p.op("scalar", lambda s, c=c, cb=cb: s.activation(
                    out=w0_bfv[:, c, :], in_=w0stg[cb], func=AF.Copy), AC)
                wcast[(oc, c)] = AC.n
            if oc > 0:
                p.wait("gpsimd", DV3, outadd[(oc - 1) * NT + NT - 1])
            p.op("gpsimd", lambda g, ocs=ocs: g.dma_start(
                out=b0_bc, in_=dm["b0"][0:1, ocs:ocs + 512]
                .broadcast_to([HD, 512])), FM, 16)
            bias_ld = FM.n

            for t in range(NT):
                tb = t % 2
                it = oc * NT + t
                p.wait("tensor", AC, wcast[(oc, ND - 1)])
                p.wait("tensor", DV, bt_cast)
                p.wait("tensor", FB, p3b_ld)
                if it >= 2:
                    p.wait("tensor", DV3, gready[it - 2])

                def p3bmm(te, t=t, tb=tb):
                    last = None
                    for c in range(ND):
                        last = te.matmul(ps_b[tb],
                                         xn3v[:, c, t * 128:(t + 1) * 128],
                                         w0_bfv[:, c, :], start=(c == 0),
                                         stop=(c == ND - 1))
                    return last
                p.op("tensor", p3bmm, TE)
                if it >= 1:
                    p.wait("tensor", DV3, blend_done[it - 1])

                def p3y(te, t=t, ocs=ocs):
                    last = None
                    for l in range(4):
                        last = te.matmul(ps_y[l],
                                         h_pad[32 * l:32 * l + 16,
                                               t * 128:(t + 1) * 128],
                                         bt_bf[32 * l:32 * l + 16,
                                               ocs:ocs + 512],
                                         start=True, stop=True,
                                         tile_position=(32 * l, 0))
                    return last
                p.op("tensor", p3y, TE)
                pe_p3[oc] = TE.n
                pe_y = TE.n

                if it >= 2:
                    p.wait("gpsimd", DV3, blend_done[it - 2])
                    p.wait("gpsimd", DV3, outadd[it - 2])
                p.op("gpsimd", lambda g, t=t, tb=tb, ocs=ocs: g.dma_start(
                    out=selc[tb],
                    in_=dm["sel"][t * 128:(t + 1) * 128, ocs:ocs + 512]),
                    FC[tb], 16)
                p.op("gpsimd", lambda g, t=t, tb=tb, ocs=ocs: g.dma_start(
                    out=xc[tb],
                    in_=dm["x"][t * 128:(t + 1) * 128, ocs:ocs + 512]),
                    FC[tb], 16)
                p.op("gpsimd", lambda g, t=t, tb=tb, ocs=ocs: g.dma_start(
                    out=sc3[tb],
                    in_=dm["s_scr"][t * 128:(t + 1) * 128, ocs:ocs + 512]),
                    FC[tb], 16)
                ld_now = FC[tb].n

                p.wait("vector", FC[tb], ld_now)
                p.wait("vector", FM, bias_ld)
                p.wait("vector", TE, pe_y)
                if it >= 1:
                    p.wait("vector", AC2, sil_done[it - 1])
                for mm, lv in ((m1, 1.0), (m2, 2.0), (m3, 3.0)):
                    p.op("vector", lambda v, mm=mm, lv=lv, tb=tb:
                         v.tensor_scalar(out=mm, in0=selc[tb], scalar1=lv,
                                         scalar2=None, op0=OP.is_equal), DV3)
                p.op("vector", lambda v: v.tensor_copy(
                    out=bl, in_=ps_y[0]), DV3)
                for mm, l in ((m1, 1), (m2, 2), (m3, 3)):
                    p.op("vector", lambda v, mm=mm, l=l: v.copy_predicated(
                        out=bl, mask=mm, data=ps_y[l]), DV3)
                blend_done[it] = DV3.n
                p.op("vector", lambda v, tb=tb: v.scalar_tensor_tensor(
                    out=Gc, in0=bl, scalar=8.0, in1=ps_b[tb],
                    op0=OP.mult, op1=OP.add), DV3)
                p.op("vector", lambda v: v.tensor_tensor(
                    out=Gc, in0=Gc, in1=b0_bc, op=OP.add), DV3)
                g_ready = DV3.n
                gready[it] = g_ready
                p.wait("scalar", DV3, g_ready)
                p.op("scalar", lambda s: s.activation(
                    out=gs, in_=Gc, func=AF.Silu), AC2)
                sil_done[it] = AC2.n
                p.wait("vector", AC2, sil_done[it])
                if it >= 2:
                    p.wait("vector", SO[it % 2], ostore[it - 2])
                p.op("vector", lambda v, tb=tb: v.tensor_tensor(
                    out=sgc, in0=sc3[tb], in1=gs, op=OP.mult), DV3)
                p.wait("vector", DV3)
                if dump_g:
                    p.op("vector", lambda v, tb=tb: v.tensor_scalar(
                        out=outc[tb], in0=Gc, scalar1=1.0, scalar2=None,
                        op0=OP.mult), DV3)
                else:
                    p.op("vector", lambda v, tb=tb: v.tensor_tensor(
                        out=outc[tb], in0=sgc, in1=xc[tb], op=OP.add), DV3)
                outadd[it] = DV3.n
                p.wait("gpsimd", DV3, outadd[it])
                p.op("gpsimd", lambda g, t=t, tb=tb, ocs=ocs: g.dma_start(
                    out=dm["out"][t * 128:(t + 1) * 128, ocs:ocs + 512],
                    in_=outc[tb]), SO[it % 2], 16)
                ostore[it] = SO[it % 2].n

        return _finish(nc, p, ALL_DMA)


def _finish(nc, p, all_dma):
    if True:
        for c in all_dma:
            p.wait("gpsimd", c)

        with nc.Block() as block:
            @block.tensor
            def _(e):
                for fn in p.streams["tensor"]:
                    fn(e)

            @block.vector
            def _(e):
                for fn in p.streams["vector"]:
                    fn(e)

            @block.scalar
            def _(e):
                for fn in p.streams["scalar"]:
                    fn(e)

            @block.sync
            def _(e):
                for fn in p.streams["sync"]:
                    fn(e)

            @block.gpsimd
            def _(e):
                for fn in p.streams["gpsimd"]:
                    fn(e)
    return nc


def _prep_weights(inputs):
    f32c = lambda a: np.ascontiguousarray(np.asarray(a), dtype=np.float32)
    def blocks(w):
        a = np.asarray(w, dtype=np.float32).reshape(ND, HD, ND, HD)
        return np.ascontiguousarray(a.transpose(0, 3, 2, 1))
    d = {}
    d["wqT"] = blocks(inputs["L0_attn_wq"])
    d["wkT"] = blocks(inputs["L0_attn_wk"])
    d["wvT"] = blocks(inputs["L0_attn_wv"])
    def ocstripes(w):
        a = np.asarray(w, dtype=np.float32).T.reshape(ND, HD, 8, 512)
        return np.ascontiguousarray(a.transpose(2, 0, 1, 3))
    d["woT"] = ocstripes(inputs["L0_attn_wo"])
    d["w0T"] = ocstripes(inputs["L0_ffn_W0"])
    A = np.asarray(inputs["L0_ffn_A"], dtype=np.float32)
    B = np.asarray(inputs["L0_ffn_B"], dtype=np.float32)
    acT = np.zeros((D, HD), np.float32)
    btP = np.zeros((HD, D), np.float32)
    for l in range(4):
        acT[:, 32 * l:32 * l + 16] = A[l].T
        btP[32 * l:32 * l + 16, :] = B[l].T
    d["acT"] = np.ascontiguousarray(
        acT.reshape(ND, HD, HD).transpose(1, 0, 2))
    d["btP"] = np.ascontiguousarray(btP)
    bqk = np.zeros((HD, 96), np.float32)
    bqk[:, 0:32] = np.asarray(inputs["L0_attn_bq"], np.float32).reshape(ND, HD).T
    bqk[:, 32:64] = np.asarray(inputs["L0_attn_bk"], np.float32).reshape(ND, HD).T
    bqk[:, 64:96] = np.asarray(inputs["L0_attn_bv"], np.float32).reshape(ND, HD).T
    d["bqk"] = np.ascontiguousarray(bqk)
    d["bo"] = f32c(np.asarray(inputs["L0_attn_bo"]).reshape(1, D))
    d["b0"] = f32c(np.asarray(inputs["L0_ffn_b0"]).reshape(1, D))
    return d


def kernel(**inputs):
    x = np.ascontiguousarray(np.asarray(inputs["L0_attn_hidden_states"],
                                        dtype=np.float32))
    wd = _prep_weights(inputs)
    sel = make_sel()
    nc = build_program(debug=False)
    in_maps = []
    for b in range(8):
        m = dict(wd)
        m["x"] = np.ascontiguousarray(x[b])
        m["sel"] = np.ascontiguousarray(sel[b])
        in_maps.append(m)
    res = run_bass_kernel_spmd(nc, in_maps, core_ids=list(range(8)))
    return np.stack([res.results[b]["out"] for b in range(8)], axis=0)
